# version 23
# baseline (speedup 1.0000x reference)
import numpy as np
from contextlib import ExitStack

import concourse.bass as bass
import concourse.mybir as mybir
from concourse.alu_op_type import AluOpType
from concourse.bass_utils import run_bass_kernel_spmd

F32 = mybir.dt.float32
F32R = mybir.dt.float32r
BF16 = mybir.dt.bfloat16
AF = mybir.ActivationFunctionType
AX = mybir.AxisListType.X
ADD = AluOpType.add
SUB = AluOpType.subtract
MULT = AluOpType.mult
MAXOP = AluOpType.max
ISGT = AluOpType.is_gt
ABSMAX = AluOpType.abs_max

D = 1024
S = 2048
NSEQ = 2
H = 16
DH = 64
FF = 4096
NDC = D // 128
NKT = S // 128
NW = S // 512
NPAIR = H // 2
EPS_RMS = float(np.float32(1.1920929e-07))
EPS_LN = 1e-5

KW_ENABLE = True
EARLY_EVAC = True

TOP = 229344
XBUF_OFF = TOP - 65536
R = XBUF_OFF - 109056
A_NORMED = R + 0
A_ZBUF = R + 32768
A_QP = R + 65536
A_KP = R + 69632
A_VP = R + 73728
A_WQKV = R + 77824
A_ABUF = R + 90112
A_VT = R + 98304
A_SQB = R + 98560
A_WOT = R + 77824
F_WCOMB = R + 0
F_R2 = R + 65536
F_WPROJ = R + 98304
ST_INVR = R + 102400
ST_B = R + 104448
ST_SQ = R + 106496
P_WLOAD = R + 0
P_AW2 = R + 32768
P_SCF = R + 49152
P_SCB = R + 65536
P_WEFF = R + 73728
P_UTB = R + 82944
P_T1B = R + 91136
P_V16B = R + 93184
P_FBTB = R + 95232
P_LRB = R + 95296


def ts(i, n):
    return slice(i * n, (i + 1) * n)


class MK:
    def __init__(self, gate):
        self.gate = float(gate)
        nc = bass.Bass("TRN2", target_bir_lowering=False)
        self.nc = nc
        dt = nc.dram_tensor
        self.xT = dt("xT", [NSEQ, D, S], F32, kind="ExternalInput")
        self.wqkvT = dt("wqkvT", [D, 3 * D], F32, kind="ExternalInput")
        self.bqkv = dt("bqkv", [3 * D], F32, kind="ExternalInput")
        self.woT_d = dt("woT", [D, D], F32, kind="ExternalInput")
        self.bo_d = dt("bo", [D], F32, kind="ExternalInput")
        self.wfcT = dt("wfcT", [D, FF], F32, kind="ExternalInput")
        self.wprojT = dt("wprojT", [FF, D], F32, kind="ExternalInput")
        self.uT_d = dt("uT", [16, FF], F32, kind="ExternalInput")
        self.v16_d = dt("v16", [16, D], F32, kind="ExternalInput")
        self.fbT_d = dt("fbT", [16, 16], F32, kind="ExternalInput")
        self.wfcN = dt("wfcN", [FF, D], F32, kind="ExternalInput")
        self.wprojN = dt("wprojN", [D, FF], F32, kind="ExternalInput")
        self.masks_d = dt("masks", [128, 4 * 512], F32, kind="ExternalInput")
        self.ident_d = dt("ident", [128, 128], F32, kind="ExternalInput")
        self.outT = dt("outT", [NSEQ, D, S], F32, kind="ExternalOutput")
        self.s_qkvT = dt("s_qkvT", [D, 3 * D], BF16, kind="Internal")
        self.s_woT = dt("s_woT", [D, D], BF16, kind="Internal")
        self.s_comb = dt("s_comb", [D, FF], BF16, kind="Internal")
        self.s_proj = dt("s_proj", [FF, D], BF16, kind="Internal")
        self.s_scale = dt("s_scale", [FF], F32, kind="Internal")

        at = nc.alloc_sbuf_tensor_at
        self.xb = at("xb", [128, NDC * S], F32, offset=XBUF_OFF)
        self.xr = at("xr", [128, NDC * S], F32R, offset=XBUF_OFF)
        self.normed = at("normed", [128, NDC * S], BF16, offset=A_NORMED)
        self.zbuf = at("zbuf", [128, NDC * S], BF16, offset=A_ZBUF)
        self.qp = at("qp", [128, S], BF16, offset=A_QP)
        self.kp = at("kp", [128, S], BF16, offset=A_KP)
        self.vp = at("vp", [128, S], BF16, offset=A_VP)
        self.wqkv = [at("wqkv%d" % i, [128, 3 * D], BF16, offset=A_WQKV + i * 6144)
                     for i in range(2)]
        self.abuf = [at("abuf%d" % i, [128, S], BF16, offset=off)
                     for i, off in enumerate((A_ABUF, A_ABUF + 4096,
                                              A_VT + 512, A_VT + 512 + 4096))]
        self.vt = at("vt", [128, 256], BF16, offset=A_VT)
        self.wot = at("wot", [128, NDC * D], BF16, offset=A_WOT)
        self.wcomb = at("wcomb", [128, NDC * FF], BF16, offset=F_WCOMB)
        self.r2 = at("r2", [128, 32 * 512], BF16, offset=F_R2)
        self.wpr0 = at("wpr0", [128, 4 * D], BF16, offset=F_WPROJ)
        self.rl = [at("rla", [128, 512], BF16, offset=ST_SQ),
                   at("rlb", [128, 512], BF16, offset=ST_SQ + 1024)]
        self.invr = at("invr", [128, 512], F32, offset=ST_INVR)
        self.stb = at("stb", [128, 512], F32, offset=ST_B)
        self.sq = [at("sqa", [128, 512], F32R, offset=ST_SQ)]
        self.wload = [at("wload%d" % i, [128, 4096], F32, offset=P_WLOAD + i * 16384)
                      for i in range(2)]
        self.aw2 = at("aw2", [128, 4096], F32R, offset=P_AW2)
        self.aw2f = at("aw2f", [128, 4096], F32, offset=P_AW2)
        self.scf = at("scf", [128, 4096], F32, offset=P_SCF)
        self.scb = at("scb", [128, 4096], BF16, offset=P_SCB)
        self.weff = at("weff", [128, 4096], BF16, offset=P_WEFF)
        self.utb = at("utb", [16, 4096], BF16, offset=P_UTB)
        self.t1b = at("t1b", [16, 1024], BF16, offset=P_T1B)
        self.v16b = at("v16b", [16, 1024], BF16, offset=P_V16B)
        self.fbtb = at("fbtb", [16, 16], BF16, offset=P_FBTB)
        self.lrb = at("lrb", [128, 4096], BF16, offset=P_LRB)

        self.es = ExitStack()
        sb = lambda n, sh, d: self.es.enter_context(nc.sbuf_tensor(n, sh, d))
        self.maskb = sb("maskb", [128, 4 * 512], BF16)
        self.identb = sb("identb", [128, 128], BF16)
        self.ones_r = sb("ones_rt", [128, 128], F32R)
        self.bqkv_s = sb("bqkv_s", [128, 24], F32)
        self.bo_s = sb("bo_s", [128, 8], F32)
        self.bvb = sb("bvb", [128, D], F32)
        self.negc = sb("negc", [128, 2], F32)
        self.zta = sb("zta", [128, 2], F32)
        self.ztb = sb("ztb", [128, 2], F32)
        self.zts = sb("zts", [128, 2], F32)
        self.ztm = sb("ztm", [128, 8], F32)
        self.rz = sb("rz", [128, 2], F32)
        self.eps_rms = sb("eps_rms", [128, 1], F32)
        self.eps_ln = sb("eps_ln", [128, 1], F32)
        self.surv = sb("surv", [128, 1024], F32)
        self.errs = sb("errs", [128, 1024], F32)
        self.scaleN = sb("scaleN", [128, 32], F32)
        self.x2bw = sb("x2bw", [128, 4096], BF16)
        self.lnm = [sb("lnm%d" % i, [128, 512], F32) for i in range(2)]
        self.lnr = [sb("lnr%d" % i, [128, 512], F32) for i in range(2)]
        self.tmp1 = sb("tmp1", [128, 512], F32)
        self.ldst = sb("ldst", [128, 512], F32)
        self.zkw = at("zkw", [128, 128], BF16, offset=ST_SQ + 2048)

        self.psA = self.es.enter_context(nc.psum_tensor("psA", [128, 2048], F32))
        self.psB = self.es.enter_context(nc.psum_tensor("psB", [128, 2048], F32))

        self.sems = {}
        self.cnt = {}
        for name in ["st", "sv", "ss", "swA", "swB", "swC", "swD", "sgx", "sgs", "so"]:
            self.sems[name] = self.es.enter_context(nc.semaphore(name))
            self.cnt[name] = 0
        self.block = self.es.enter_context(nc.Block())

    class _MockInstr:
        __slots__ = ("ops",)

        def __init__(self, ops):
            self.ops = ops

        def then_inc(self, sem, k=1):
            self.ops.append(("__then_inc__", (sem, k), {}))
            return self


    class _Rec:

        def __init__(self, ops):
            self._ops = ops

        def __getattr__(self, name):
            ops = self._ops

            def call(*a, **kw):
                ops.append((name, a, kw))
                return MK._MockInstr(ops)
            return call

    def xf(self, dc, w):
        return self.xb[:, ts(dc, S)][:, ts(w, 512)]

    def xfr(self, dc, w):
        return self.xr[:, ts(dc, S)][:, ts(w, 512)]

    def yf(self, dc, w):
        return self.xr[:, ts(dc, S)][:, ts(w, 512)].bitcast(F32)

    def _rec(self, key, f):
        if not hasattr(self, "_oplists"):
            self._oplists = {k: [] for k in ("tensor", "vector", "scalar", "sync", "gpsimd")}
        f(MK._Rec(self._oplists[key]))

    def T(self, f):
        self._rec("tensor", f)

    def V(self, f):
        self._rec("vector", f)

    def Sc(self, f):
        self._rec("scalar", f)

    def Sy(self, f):
        self._rec("sync", f)

    def G(self, f):
        self._rec("gpsimd", f)

    def _replay_all(self):
        def replay(ops, eng):
            last = None
            for (name, a, kw) in ops:
                if name == "__then_inc__":
                    last.then_inc(*a)
                else:
                    last = getattr(eng, name)(*a, **kw)
        self.block.tensor(lambda e: replay(self._oplists["tensor"], e))
        self.block.vector(lambda e: replay(self._oplists["vector"], e))
        self.block.scalar(lambda e: replay(self._oplists["scalar"], e))
        self.block.sync(lambda e: replay(self._oplists["sync"], e))
        self.block.gpsimd(lambda e: replay(self._oplists["gpsimd"], e))

    def sync_group(self, emit_fn, extra_waits=()):
        if not hasattr(self, "_ring"):
            self._ring = ["swA", "swB", "swC", "swD"]
            self._ring_i = 0
            self._ring_prev = {}
        sname = self._ring[self._ring_i % 4]
        self._ring_i += 1
        prev = self._ring_prev.get(sname)
        mk, sm = self, self.sems

        def sy(sync):
            if prev is not None:
                for (tn, tv) in prev["consumers"]:
                    mk.w(sync, sm, tn, tv)
            for (tn, tv) in extra_waits:
                mk.w(sync, sm, tn, tv)
            emit_fn(sync, lambda instr: mk.inc(sname, instr, 16))
        self.Sy(sy)
        ref = {"sem": sname, "tok": self.cnt[sname], "consumers": []}
        self._ring_prev[sname] = ref
        return ref

    @staticmethod
    def ref_consumed(ref, sem, tok):
        ref["consumers"].append((sem, tok))

    def inc(self, name, instr, k=1):
        instr.then_inc(self.sems[name], k)
        self.cnt[name] += k
        return self.cnt[name]

    @staticmethod
    def w(eng, sem_map, name, tok):
        if tok > 0:
            eng.wait_ge(sem_map[name], tok)

    def build(self):
        self.emit_consts()
        self.emit_phase0()
        for sq in range(NSEQ):
            self.emit_seq(sq)
        self.emit_final()
        self._replay_all()
        self.es.close()
        return self.nc

    def emit_consts(self):
        mk, sm = self, self.sems

        def dmas(sync, incf):
            incf(sync.dma_start(mk.wload[0][:, 0:2048], mk.masks_d[:, :]))
            incf(sync.dma_start(mk.ldst[:, 0:128], mk.ident_d[:, :]))
            incf(sync.dma_start(mk.bqkv_s[:].rearrange("p (a c) -> p a c", c=1),
                                mk.bqkv.rearrange("(a p) -> p a", p=128).unsqueeze(2),
                                allow_slow_non_contiguous=True))
            incf(sync.dma_start(mk.bo_s[:].rearrange("p (a c) -> p a c", c=1),
                                mk.bo_d.rearrange("(a p) -> p a", p=128).unsqueeze(2),
                                allow_slow_non_contiguous=True))
            incf(sync.dma_start(mk.bvb[0:1, :], mk.bqkv[2048:3072].unsqueeze(0)))
        ref = self.sync_group(dmas)

        def g(gp):
            mk.w(gp, sm, ref["sem"], ref["tok"])
            rows = 1
            while rows < 128:
                nxt = min(2 * rows, 128)
                i = gp.dma_start(mk.bvb[rows:nxt, :], mk.bvb[0:rows, :][0:nxt - rows, :])
                tok = mk.inc("sgx", i, 16)
                gp.wait_ge(sm["sgx"], tok)
                rows = nxt
            for dc in range(NDC):
                i = gp.dma_start(mk.xb[:, ts(dc, S)], mk.xT[0, ts(dc, 128), :])
                mk.inc("sgx", i, 16)
        self.G(g)
        self.xload_tok = [self.cnt["sgx"], None]
        self.ref_consumed(ref, "sgx", self.cnt["sgx"])

        def v(vector):
            mk.w(vector, sm, ref["sem"], ref["tok"])
            vector.tensor_copy(mk.maskb[:, :], mk.wload[0][:, 0:2048])
            vector.tensor_copy(mk.identb[:, :], mk.ldst[:, 0:128])
            vector.memset(mk.tmp1[:, 0:128], 1.0)
            vector.memset(mk.eps_rms[:, :], EPS_RMS)
            vector.memset(mk.eps_ln[:, :], EPS_LN)
            vector.memset(mk.zkw[:, :], 0.0)
            vector.drain()
            i = vector.tensor_copy(mk.ones_r[:, :], mk.tmp1[:, 0:128])
            mk.inc("sv", i)
        self.V(v)
        self.consts_tok = self.cnt["sv"]
        self.ref_consumed(ref, "sv", self.consts_tok)

    def cast_to_scratch(self, src, dst, ncols, wait_first):
        mk, sm = self, self.sems
        total = NDC * 128 * ncols
        tile = 128 * 4096
        nt = total // tile
        srcf = src[:, :].flatten()
        dstf = dst[:, :].flatten()
        store_tok = {-1: self.cnt["sgs"]}
        cast_tok = {}
        for dc in range(nt):
            par = dc % 2
            extra = [("sv", cast_tok[dc - 2])] if dc >= 2 else (
                [("sv", wait_first)] if wait_first else [])
            ref = self.sync_group(
                lambda sync, incf, dc=dc, par=par: incf(
                    sync.dma_start(mk.wload[par][:, 0:4096],
                                   srcf[dc * tile:(dc + 1) * tile]
                                   .rearrange("(p c) -> p c", p=128))),
                extra_waits=extra)

            def v(vector, dc=dc, par=par, ref=ref):
                mk.w(vector, sm, ref["sem"], ref["tok"])
                mk.w(vector, sm, "sgs", store_tok[dc - 1])
                i = vector.tensor_copy(mk.weff[:, 0:4096], mk.wload[par][:, 0:4096])
                mk.inc("sv", i)
            self.V(v)
            cast_tok[dc] = self.cnt["sv"]
            self.ref_consumed(ref, "sv", cast_tok[dc])

            def g(gp, dc=dc):
                mk.w(gp, sm, "sv", cast_tok[dc])
                i = gp.dma_start(dstf[dc * tile:(dc + 1) * tile]
                                 .rearrange("(p c) -> p c", p=128),
                                 mk.weff[:, 0:4096])
                mk.inc("sgs", i, 16)
            self.G(g)
            store_tok[dc] = self.cnt["sgs"]
        return store_tok[nt - 1]

    def emit_phase0(self):
        mk, sm = self, self.sems
        self.qkvT_store = self.cast_to_scratch(self.wqkvT, self.s_qkvT, 3 * D,
                                               self.consts_tok)
        self.woT_store = self.cast_to_scratch(self.woT_d, self.s_woT, D, self.cnt["sv"])

        svsnap0 = self.cnt["sv"]

        def dmas(sync, incf):
            incf(sync.dma_start(mk.wload[0][0:16, 0:1024], mk.v16_d[:, :]))
            incf(sync.dma_start(mk.wload[0][0:16, 1024:1040], mk.fbT_d[:, :]))
            incf(sync.dma_start(mk.wload[1][0:16, 0:4096], mk.uT_d[:, :]))
        ref = self.sync_group(dmas, extra_waits=[("sv", svsnap0)])

        def v(vector):
            mk.w(vector, sm, ref["sem"], ref["tok"])
            vector.drain()
            vector.tensor_copy(mk.v16b[:, :], mk.wload[0][0:16, 0:1024])
            vector.tensor_copy(mk.fbtb[:, :], mk.wload[0][0:16, 1024:1040])
            i = vector.tensor_copy(mk.utb[:, :], mk.wload[1][0:16, 0:4096])
            mk.inc("sv", i)
        self.V(v)
        lr_cast = self.cnt["sv"]
        self.ref_consumed(ref, "sv", lr_cast)

        def t(tensor):
            mk.w(tensor, sm, "sv", lr_cast)
            tensor.matmul(mk.psA[0:16, 0:512], mk.fbtb[:, :], mk.v16b[:, 0:512],
                          start=True, stop=True)
            i = tensor.matmul(mk.psA[0:16, 512:1024], mk.fbtb[:, :],
                              mk.v16b[:, 512:1024], start=True, stop=True)
            mk.inc("st", i)
        self.T(t)
        t1_mm = self.cnt["st"]

        def v2(vector):
            mk.w(vector, sm, "st", t1_mm)
            i = vector.tensor_copy(mk.t1b[:, :], mk.psA[0:16, 0:1024])
            mk.inc("sv", i)
        self.V(v2)
        self.t1_tok = self.cnt["sv"]

        self.ternarize(self.wfcT, self.wfcN, NDC, FF, 32, D, self.s_comb, with_lr=True)
        self.ternarize(self.wprojT, self.wprojN, FF // 128, D, 8, FF, self.s_proj, with_lr=False)

    def ternarize(self, wsrcT, wsrcN, nchunks, ncols, ntiles, ncolsN, sdst, with_lr):
        mk, sm = self, self.sems
        ngr = ncols // 512
        sv0 = self.cnt["sv"]
        ss0 = self.cnt["ss"]
        lvl0 = 32
        tps = 4096 // ncolsN
        nst = ntiles // tps
        tile_tok = {}
        for st_i in range(nst):
            par = st_i % 2
            extra = ([("sv", tile_tok[st_i - 2])] if st_i >= 2
                     else [("sv", sv0), ("ss", ss0)])
            ref = self.sync_group(
                lambda sync, incf, st_i=st_i, par=par: incf(
                    sync.dma_start(
                        mk.wload[par][:, 0:4096].rearrange("b (a c) -> b a c", a=tps),
                        wsrcN.rearrange("(a b) c -> b a c", b=128)
                        [:, st_i * tps:(st_i + 1) * tps, :])),
                extra_waits=extra)

            def sabs(scalar, st_i=st_i, par=par, ref=ref):
                mk.w(scalar, sm, ref["sem"], ref["tok"])
                mk.w(scalar, sm, "sv", mk.cnt["sv"])
                i = scalar.activation(out=mk.aw2f[:, 0:4096],
                                      in_=mk.wload[par][:, 0:4096],
                                      func=AF.Abs, bias=0.0, scale=1.0)
                mk.inc("ss", i)
            self.Sc(sabs)
            abs_tok = self.cnt["ss"]

            def vtree(vector, st_i=st_i, abs_tok=abs_tok):
                mk.w(vector, sm, "ss", abs_tok)
                a3 = mk.aw2f[:, 0:4096].rearrange("p (t c) -> p t c", c=ncolsN)
                n = ncolsN
                while n > lvl0:
                    h = n // 2
                    vector.tensor_tensor(a3[:, :, 0:h], a3[:, :, 0:h],
                                         a3[:, :, h:n], op=ADD)
                    vector.drain()
                    n = h
                i = vector.tensor_copy(
                    mk.surv[:, st_i * tps * lvl0:(st_i + 1) * tps * lvl0]
                    .rearrange("p (t c) -> p t c", c=lvl0),
                    a3[:, :, 0:lvl0])
                mk.inc("sv", i)
            self.V(vtree)
            tile_tok[st_i] = self.cnt["sv"]

        def vcomp2(vector):
            mk.w(vector, sm, "sv", tile_tok[nst - 1])
            T = ntiles
            sv_ = lambda c0, cn: mk.surv[:, 0:T * lvl0].rearrange(
                "p (t c) -> p t c", c=lvl0)[:, :, c0:c0 + cn]
            a3 = lambda c0, cn, h: mk.aw2f[:, c0:c0 + T * h].rearrange(
                "p (t c) -> p t c", c=h)
            eoff = 0
            err_cols = []
            n = lvl0
            while n > 1:
                h = n // 2
                a = sv_(0, h)
                b = sv_(h, n - h)
                stmp = mk.aw2f[:, 0:T * h].rearrange("p (t c) -> p t c", c=h)
                ztmp = mk.aw2f[:, 512:512 + T * h].rearrange("p (t c) -> p t c", c=h)
                wtmp = mk.aw2f[:, 1024:1024 + T * h].rearrange("p (t c) -> p t c", c=h)
                e1 = mk.aw2f[:, 1536:1536 + T * h].rearrange("p (t c) -> p t c", c=h)
                e2 = mk.aw2f[:, 2048:2048 + T * h].rearrange("p (t c) -> p t c", c=h)
                vector.tensor_tensor(stmp, a, b, op=ADD)
                vector.drain()
                vector.tensor_tensor(ztmp, stmp, a, op=SUB)
                vector.drain()
                vector.tensor_tensor(wtmp, stmp, ztmp, op=SUB)
                vector.drain()
                vector.tensor_tensor(e1, a, wtmp, op=SUB)
                vector.drain()
                vector.tensor_tensor(e2, b, ztmp, op=SUB)
                vector.drain()
                etot = mk.errs[:, eoff:eoff + T * h].rearrange("p (t c) -> p t c", c=h)
                vector.tensor_tensor(etot, e1, e2, op=ADD)
                vector.drain()
                err_cols.append((eoff, h))
                eoff += T * h
                vector.tensor_copy(sv_(0, h), stmp)
                vector.drain()
                n = h
            acc = mk.tmp1[:, 0:T]
            red = mk.tmp1[:, 64:64 + T]
            first = True
            for (c0, h) in err_cols:
                lv = mk.errs[:, c0:c0 + T * h].rearrange("p (t c) -> p t c", c=h)
                if h == 1:
                    src = mk.errs[:, c0:c0 + T]
                else:
                    vector.tensor_reduce(red, lv, axis=AX, op=ADD)
                    vector.drain()
                    src = red
                if first:
                    vector.tensor_copy(acc, src)
                else:
                    vector.tensor_tensor(acc, acc, src, op=ADD)
                vector.drain()
                first = False
            root = sv_(0, 1).squeeze(2)
            vector.tensor_tensor(mk.tmp1[:, 128:128 + T], root, acc, op=ADD)
            vector.drain()
            i = vector.tensor_scalar(mk.scaleN[:, 0:ntiles], mk.tmp1[:, 128:128 + T],
                                     1.0 / ncolsN, 1e-5, op0=MULT, op1=MAXOP)
            mk.inc("sv", i)
        self.V(vcomp2)
        scaleN_tok = self.cnt["sv"]

        sgs_snap = self.cnt["sgs"]

        def g1(gp):
            mk.w(gp, sm, "sv", scaleN_tok)
            gp.wait_ge(sm["sgs"], sgs_snap)
            i = gp.dma_start(mk.s_scale[0:ncols].rearrange("(t p) -> p t", p=128),
                             mk.scaleN[:, 0:ntiles], allow_slow_non_contiguous=True)
            tok = mk.inc("sgs", i, 16)
            gp.wait_ge(sm["sgs"], tok)
            i = gp.dma_start(mk.scf[0:1, 0:ncols], mk.s_scale[0:ncols].unsqueeze(0))
            tok = mk.inc("sgs", i, 16)
            gp.wait_ge(sm["sgs"], tok)
            rows = 1
            while rows < 128:
                nxt = min(2 * rows, 128)
                i = gp.dma_start(mk.scf[rows:nxt, 0:ncols],
                                 mk.scf[0:rows, 0:ncols][0:nxt - rows, :])
                tok = mk.inc("sgs", i, 16)
                gp.wait_ge(sm["sgs"], tok)
                rows = nxt
        self.G(g1)
        bcast_tok = self.cnt["sgs"]

        def vscb(vector):
            mk.w(vector, sm, "sgs", bcast_tok)
            i = vector.tensor_copy(mk.scb[:, 0:ncols], mk.scf[:, 0:ncols])
            mk.inc("sv", i)
        self.V(vscb)
        scale_tok = self.cnt["sv"]

        app_tok = {}
        store_tok = {-1: self.cnt["sgs"], -2: self.cnt["sgs"]}
        lr_tok = {}
        for c in range(nchunks):
            par = c % 2
            extra = [("sv", scale_tok if c < 2 else app_tok[c - 2]),
                     ("sgs", store_tok[c - 2])]
            ref = self.sync_group(
                lambda sync, incf, c=c, par=par: incf(
                    sync.dma_start(mk.wload[par][:, 0:ncols], wsrcT[ts(c, 128), :])),
                extra_waits=extra)
            ldn, lt = ref["sem"], ref["tok"]
            if with_lr:
                sgn = mk.weff[:, 0:ncols]
                res = mk.lrb[:, 0:ncols]
            else:
                sgn = mk.wload[par][:, 2048:4096].bitcast(BF16)[:, 0:ncols]
                res = sgn

            def s(scalar, c=c, par=par, ldn=ldn, lt=lt, sgn=sgn):
                mk.w(scalar, sm, ldn, lt)
                if c >= 1:
                    mk.w(scalar, sm, "sv", app_tok[c - 1])
                mk.w(scalar, sm, "sv", scale_tok)
                scalar.activation(out=mk.aw2f[:, 0:ncols],
                                  in_=mk.wload[par][:, 0:ncols],
                                  func=AF.Abs, bias=0.0, scale=2.0)
                i = scalar.activation(out=sgn,
                                      in_=mk.wload[par][:, 0:ncols],
                                      func=AF.Sign, bias=0.0, scale=1.0)
                mk.inc("ss", i)
            self.Sc(s)
            sgn_tok = self.cnt["ss"]

            if with_lr:
                def t(tensor, c=c):
                    mk.w(tensor, sm, "sv", mk.t1_tok)
                    if c == 0:
                        mk.w(tensor, sm, "sv", scale_tok)
                    if c >= 1:
                        mk.w(tensor, sm, "ss", lr_tok[c - 1])
                    for g in range(ngr):
                        i = tensor.matmul(mk.psB[:, ts(g % 4, 512)],
                                          mk.t1b[:, ts(c, 128)], mk.utb[:, ts(g, 512)],
                                          start=True, stop=True)
                    mk.inc("st", i)
                self.T(t)
                wlr_tok = self.cnt["st"]

                def s2(scalar, c=c, wlr_tok=wlr_tok):
                    mk.w(scalar, sm, "st", wlr_tok)
                    if c >= 1:
                        mk.w(scalar, sm, "sv", app_tok[c - 1])
                        mk.w(scalar, sm, "sgs", store_tok[c - 1])
                    for g in range(ngr):
                        i = scalar.activation(out=mk.lrb[:, ts(g, 512)],
                                              in_=mk.psB[:, ts(g % 4, 512)],
                                              func=AF.Copy, bias=0.0, scale=mk.gate)
                    mk.inc("ss", i)
                self.Sc(s2)
                lr_tok[c] = self.cnt["ss"]

            def v2(vector, c=c, par=par, ldn=ldn, lt=lt, sgn_tok=sgn_tok,
                   sgn=sgn, res=res):
                mk.w(vector, sm, ldn, lt)
                mk.w(vector, sm, "ss", sgn_tok if not with_lr else lr_tok[c])
                vector.drain()
                mb = [mk.rl[0][:, 0:512], mk.rl[1][:, 0:512],
                      mk.tmp1[:, 0:256].bitcast(BF16), mk.tmp1[:, 256:512].bitcast(BF16)]
                i = None
                for g0 in range(0, ngr, 4):
                    gs = range(g0, min(g0 + 4, ngr))
                    for g in gs:
                        vector.tensor_tensor(mb[g - g0], mk.aw2f[:, ts(g, 512)],
                                             mk.scf[:, ts(g, 512)], op=ISGT)
                    vector.drain()
                    for g in gs:
                        vector.tensor_tensor(mb[g - g0], mb[g - g0],
                                             sgn[:, ts(g, 512)], op=MULT)
                    vector.drain()
                    if with_lr:
                        for g in gs:
                            vector.tensor_tensor(mb[g - g0], mb[g - g0],
                                                 mk.scb[:, ts(g, 512)], op=MULT)
                        vector.drain()
                        for g in gs:
                            i = vector.tensor_tensor(res[:, ts(g, 512)], mb[g - g0],
                                                     mk.lrb[:, ts(g, 512)], op=ADD)
                    else:
                        for g in gs:
                            i = vector.tensor_tensor(res[:, ts(g, 512)], mb[g - g0],
                                                     mk.scb[:, ts(g, 512)], op=MULT)
                    if g0 + 4 < ngr:
                        vector.drain()
                mk.inc("sv", i)
            self.V(v2)
            app_tok[c] = self.cnt["sv"]
            self.ref_consumed(ref, "sv", app_tok[c])

            def g2(gp, c=c, res=res):
                mk.w(gp, sm, "sv", app_tok[c])
                mk.w(gp, sm, "sgs", store_tok[c - 1])
                i = gp.dma_start(sdst[ts(c, 128), :], res[:, 0:ncols])
                mk.inc("sgs", i, 16)
            self.G(g2)
            store_tok[c] = self.cnt["sgs"]
        if with_lr:
            self.comb_store = store_tok[nchunks - 1]
        else:
            self.proj_store = store_tok[nchunks - 1]

    def emit_seq(self, sq):
        mk, sm = self, self.sems
        if sq == 1:
            out_all = self.cnt["so"]

            def g(gp):
                gp.wait_ge(sm["so"], out_all)
                for dc in range(NDC):
                    i = gp.dma_start(mk.xb[:, ts(dc, S)], mk.xT[1, ts(dc, 128), :])
                    mk.inc("sgx", i, 16)
            self.G(g)
            self.xload_tok[1] = self.cnt["sgx"]

        self.emit_rms(sq)
        for p in range(NPAIR):
            self.emit_qkv_pair(sq, p)
            self.emit_attention_pair(sq, p)
        self.emit_wo_ln1(sq)
        pend = self.emit_ffn(sq)
        self._flush_ln2(pend)

    def emit_rms(self, sq):
        mk, sm = self, self.sems
        sq_tok = {}
        mm_tok = {}
        sv0 = self.cnt["sv"]
        for w in range(NW):
            for dc in range(NDC):
                slot = 0

                def s(scalar, w=w, dc=dc, slot=slot):
                    if w == 0 and dc == 0:
                        mk.w(scalar, sm, "sgx", mk.xload_tok[sq])
                        mk.w(scalar, sm, "sv", sv0)
                    if dc >= 1:
                        mk.w(scalar, sm, "st", mm_tok[(w, dc - 1)])
                    i = scalar.activation(out=mk.sq[slot][:, 0:512],
                                          in_=mk.xb[:, ts(dc, S)][:, ts(w, 512)],
                                          func=AF.Square, bias=0.0, scale=1.0)
                    mk.inc("ss", i)
                self.Sc(s)
                sq_tok[(w, dc)] = self.cnt["ss"]

                def t(tensor, w=w, dc=dc, slot=slot):
                    mk.w(tensor, sm, "ss", sq_tok[(w, dc)])
                    if w > 0 and dc == 0:
                        mk.w(tensor, sm, "ss", sqrt_tok[w - 1])
                    i = tensor.matmul(mk.psA[:, 0:512], mk.ones_r[:, :],
                                      mk.sq[slot][:, 0:512],
                                      start=(dc == 0), stop=(dc == NDC - 1))
                    mk.inc("st", i)
                self.T(t)
                mm_tok[(w, dc)] = self.cnt["st"]

            def s2(scalar, w=w):
                mk.w(scalar, sm, "st", mm_tok[(w, NDC - 1)])
                if w > 0:
                    mk.w(scalar, sm, "sv", mk.norm_tok[w - 1])
                i = scalar.activation(out=mk.stb[:, 0:512], in_=mk.psA[:, 0:512],
                                      func=AF.Sqrt, bias=mk.eps_rms[:, :], scale=1.0 / D)
                mk.inc("ss", i)
            self.Sc(s2)
            if w == 0:
                sqrt_tok = {}
            sqrt_tok[w] = self.cnt["ss"]

            def v(vector, w=w):
                mk.w(vector, sm, "ss", sqrt_tok[w])
                if sq == 0 and w == 0 and hasattr(mk, "proj_store"):
                    mk.w(vector, sm, "sgs", mk.proj_store)
                vector.reciprocal(mk.invr[:, 0:512], mk.stb[:, 0:512])
                vector.drain()
                for dc in range(NDC):
                    i = vector.tensor_tensor(mk.normed[:, ts(dc, S)][:, ts(w, 512)],
                                             mk.xb[:, ts(dc, S)][:, ts(w, 512)],
                                             mk.invr[:, 0:512], op=MULT)
                mk.inc("sv", i)
            self.V(v)
            if w == 0:
                self.norm_tok = {}
            self.norm_tok[w] = self.cnt["sv"]
        self.normed_done = self.cnt["sv"]

    def emit_qkv_pair(self, sq, p):
        mk, sm = self, self.sems
        slot = p % 2
        done_qk, done_v = set(), set()
        pre = getattr(self, "qkv_pre", None)
        if pre is not None and pre["pair"] == (sq, p):
            qref = pre["ref"]
            od = getattr(self, "ovl_done", None)
            if od is not None and od["pair"] == (sq, p):
                done_qk, done_v = od["qk"], od["v"]
        else:
            extra = []
            if sq == 0 and p < 2:
                extra.append(("sgs", mk.proj_store))
            if hasattr(mk, "qkv_mm_done") and (sq, p) != (0, 0):
                if p >= 2 and mk.qkv_mm_done.get((sq, p - 2)):
                    extra.append(("st", mk.qkv_mm_done[(sq, p - 2)]))
                if p < 2 and sq == 1 and mk.qkv_mm_done.get((0, p + NPAIR - 2)):
                    extra.append(("st", mk.qkv_mm_done[(0, p + NPAIR - 2)]))

            def dmas(sync, incf, p=p, slot=slot):
                for sec in range(3):
                    incf(sync.dma_start(
                        mk.wqkv[slot][:].rearrange("b (s a c) -> b s a c", s=3, a=NDC)
                        [:, sec:sec + 1, :, :],
                        mk.s_qkvT.rearrange("(a b) (s f) -> b s a f", b=128, s=3)
                        [:, sec:sec + 1, :, ts(p, 128)]))
            qref = self.sync_group(dmas, extra_waits=extra)
        ldn, wld = qref["sem"], qref["tok"]
        self.qref = qref
        self.ovl_done = None
        first = [True]

        def first_waits(tensor):
            mk.w(tensor, sm, ldn, wld)
            mk.w(tensor, sm, "sv", mk.normed_done)
            if (sq, p) != (0, 0):
                mk.w(tensor, sm, "ss", mk.cnt["ss"])
            first[0] = False

        qk_ev = {}
        for sec, dstbuf, bcol in ((0, self.qp, p), (1, self.kp, 8 + p)):
            for jw in range(NW):
                if (sec, jw) in done_qk:
                    continue
                bank = jw % 2
                fw = first[0]
                pred = qk_ev.get((sec, jw - 2) if jw >= 2 else (sec - 1, jw + 2))

                def t(tensor, sec=sec, jw=jw, bank=bank, fw=fw, pred=pred):
                    if fw:
                        first_waits(tensor)
                    if pred:
                        mk.w(tensor, sm, "sv", pred)
                    for dc in range(NDC):
                        i = tensor.matmul(mk.psA[:, ts(bank, 512)],
                                          mk.wqkv[p % 2][:, sec * D + dc * 128:
                                                         sec * D + dc * 128 + 128],
                                          mk.normed[:, ts(dc, S)][:, ts(jw, 512)],
                                          start=(dc == 0), stop=(dc == NDC - 1))
                    mk.inc("st", i)
                self.T(t)
                mmt = self.cnt["st"]

                def v(vector, jw=jw, bank=bank, mmt=mmt, dstbuf=dstbuf, bcol=bcol):
                    mk.w(vector, sm, "st", mmt)
                    i = vector.tensor_scalar(dstbuf[:, ts(jw, 512)],
                                             mk.psA[:, ts(bank, 512)],
                                             mk.bqkv_s[:, bcol:bcol + 1], None, op0=ADD)
                    mk.inc("sv", i)
                self.V(v)
                qk_ev[(sec, jw)] = self.cnt["sv"]

        v_ev = {}
        for tt in range(NKT):
            if tt in done_v:
                continue
            bank = tt % 2
            fw = first[0]
            pred = v_ev.get(tt - 2)

            def t(tensor, tt=tt, bank=bank, fw=fw, pred=pred):
                if fw:
                    first_waits(tensor)
                if pred:
                    mk.w(tensor, sm, "sv", pred)
                for dc in range(NDC):
                    i = tensor.matmul(mk.psA[:, 1024 + bank * 512: 1024 + bank * 512 + 128],
                                      mk.normed[:, ts(dc, S)][:, ts(tt, 128)],
                                      mk.wqkv[p % 2][:, 2 * D + dc * 128: 2 * D + dc * 128 + 128],
                                      start=(dc == 0), stop=(dc == NDC - 1))
                mk.inc("st", i)
            self.T(t)
            mmt = self.cnt["st"]

            def v(vector, tt=tt, bank=bank, mmt=mmt):
                mk.w(vector, sm, "st", mmt)
                i = vector.tensor_tensor(mk.vp[:, ts(tt, 128)],
                                         mk.psA[:, 1024 + bank * 512: 1024 + bank * 512 + 128],
                                         mk.bvb[:, ts(p, 128)], op=ADD)
                mk.inc("sv", i)
            self.V(v)
            v_ev[tt] = self.cnt["sv"]
        if not hasattr(self, "qkv_mm_done"):
            self.qkv_mm_done = {}
        self.qkv_mm_done[(sq, p)] = self.cnt["st"]
        self.ref_consumed(self.qref, "st", self.cnt["st"])
        self.qkv_ev_done = self.cnt["sv"]
        self.qk_ev_done = self.cnt["sv"]

    def emit_attention_pair(self, sq, p):
        mk, sm = self, self.sems
        first_pair = not hasattr(self, "gkt")
        if first_pair:
            self.gkt = 0
            self.av_tok = {}
            self.zevac_tok = {}
        qkv_v = self.qkv_ev_done
        qkv_s = self.qk_ev_done
        self._kw_bank = None
        self._kw_pending = None

        if first_pair or p == 0:
            def vz(vector):
                if not first_pair:
                    mk.w(vector, sm, "sv", self.zevac_tok[(sq - 1, NPAIR - 1)])
                    vector.drain()
                i = vector.memset(mk.psB[64:128, :], 0.0)
                mk.inc("sv", i)
            self.V(vz)
            zinit_tok = self.cnt["sv"]
        else:
            zinit_tok = None

        ovl_q = []
        ovl_banks = {}
        ovl_ready = {}
        ovl_first = [True]
        ovl_started = [False]
        ovl_done_qk, ovl_done_v = set(), set()
        if p + 1 < NPAIR:
            np_ = p + 1
            nslot = np_ % 2
            guard = []
            if p >= 1:
                guard.append(("st", self.qkv_mm_done[(sq, p - 1)]))
            elif sq >= 1:
                guard.append(("st", self.qkv_mm_done[(sq - 1, NPAIR - 1)]))
            if sq == 0 and np_ < 2:
                guard.append(("sgs", mk.proj_store))

            def pdmas(sync, incf, np_=np_, nslot=nslot):
                for sec in range(3):
                    incf(sync.dma_start(
                        mk.wqkv[nslot][:].rearrange("b (s a c) -> b s a c", s=3, a=NDC)
                        [:, sec:sec + 1, :, :],
                        mk.s_qkvT.rearrange("(a b) (s f) -> b s a f", b=128, s=3)
                        [:, sec:sec + 1, :, ts(np_, 128)]))
            pref = self.sync_group(pdmas, extra_waits=guard)
            self.qkv_pre = {"pair": (sq, np_), "ref": pref}
            ovl_q = [("v", t) for t in range(16)]

        UL = []
        for kt in range(NKT):
            j0 = kt >> 2
            nq = S - 512 * j0
            ups = [(512 * j0, min(1024, nq))]
            if nq > 1024:
                ups.append((512 * j0 + 1024, nq - 1024))
            for u, (qs, wdt) in enumerate(ups):
                UL.append((kt, u, qs, wdt, u == 0, u == len(ups) - 1))
        NU = len(UL)
        gkt0 = self.gkt
        sets = [mk.psA[:, 0:1024], mk.psA[:, 1024:2048]]
        exp_tok = {}
        sc_tok = {}
        vt_tok = {}
        pend_q = []

        def flush_av(need_gkt=None, max_age=2):
            while pend_q and (pend_q[0][1] >= max_age or
                              (need_gkt is not None and pend_q[0][0] <= need_gkt)):
                pend_q.pop(0)[2]()

        def age_av():
            for k in range(len(pend_q)):
                pend_q[k] = (pend_q[k][0], pend_q[k][1] + 1, pend_q[k][2])

        def emit_scores_pair(tensor, units_):
            chunks = []
            for hh, ui in units_:
                kt, u, qs, wdt, first, last = UL[ui]
                if first:
                    tensor.matmul(sets[hh][:, 0:512], mk.identb[:, :],
                                  mk.maskb[:, ts(kt % 4, 512)],
                                  start=True, stop=False)
                for n in range(wdt // 512):
                    chunks.append((hh, ui, n))
            i2 = None
            for ci in range(max(len([c for c in chunks if c[0] == 0]),
                                len([c for c in chunks if c[0] == 1]))):
                for hh in range(2):
                    hc = [c for c in chunks if c[0] == hh]
                    if ci < len(hc):
                        _, ui, n = hc[ci]
                        kt, u, qs, wdt, first, last = UL[ui]
                        po = 64 * hh
                        i2 = tensor.matmul(
                            sets[hh][:, ts(n, 512)],
                            mk.kp[po:po + 64, ts(kt, 128)],
                            mk.qp[po:po + 64, qs + 512 * n: qs + 512 * n + 512],
                            start=(not (first and n == 0)), stop=True)
            return i2

        def emit_ovl_group(g, bank, wtok):
            np_ = p + 1
            nslot = np_ % 2
            pref = self.qkv_pre["ref"]
            fw = ovl_first[0]
            ovl_first[0] = False
            ovl_started[0] = True
            if g[0] == "qk":
                _, sec, w = g

                def t(tensor, sec=sec, w=w, bank=bank, wtok=wtok, fw=fw):
                    if fw:
                        mk.w(tensor, sm, pref["sem"], pref["tok"])
                    if wtok:
                        mk.w(tensor, sm, "sv", wtok)
                    for dc in range(NDC):
                        i2 = tensor.matmul(mk.psB[:, ts(bank, 512)],
                                           mk.wqkv[nslot][:, sec * D + dc * 128:
                                                          sec * D + dc * 128 + 128],
                                           mk.normed[:, ts(dc, S)][:, ts(w, 512)],
                                           start=(dc == 0), stop=(dc == NDC - 1))
                    mk.inc("st", i2)
                self.T(t)
                mmt = self.cnt["st"]

                def v(vector, sec=sec, w=w, bank=bank, mmt=mmt):
                    mk.w(vector, sm, "st", mmt)
                    dst = mk.qp if sec == 0 else mk.kp
                    bcol = np_ if sec == 0 else 8 + np_
                    i2 = vector.tensor_scalar(dst[:, ts(w, 512)],
                                              mk.psB[:, ts(bank, 512)],
                                              mk.bqkv_s[:, bcol:bcol + 1], None,
                                              op0=ADD)
                    mk.inc("sv", i2)
                self.V(v)
                ovl_done_qk.add((sec, w))
            else:
                _, tt = g

                def t(tensor, tt=tt, bank=bank, wtok=wtok, fw=fw):
                    if fw:
                        mk.w(tensor, sm, pref["sem"], pref["tok"])
                    if wtok:
                        mk.w(tensor, sm, "sv", wtok)
                    for dc in range(NDC):
                        i2 = tensor.matmul(mk.psB[:, bank * 512: bank * 512 + 128],
                                           mk.normed[:, ts(dc, S)][:, ts(tt, 128)],
                                           mk.wqkv[nslot][:, 2 * D + dc * 128:
                                                          2 * D + dc * 128 + 128],
                                           start=(dc == 0), stop=(dc == NDC - 1))
                    mk.inc("st", i2)
                self.T(t)
                mmt = self.cnt["st"]

                def v(vector, tt=tt, bank=bank, mmt=mmt):
                    mk.w(vector, sm, "st", mmt)
                    i2 = vector.tensor_tensor(mk.vp[:, ts(tt, 128)],
                                               mk.psB[:, bank * 512: bank * 512 + 128],
                                               mk.bvb[:, ts(np_, 128)], op=ADD)
                    mk.inc("sv", i2)
                self.V(v)
                ovl_done_v.add(tt)
            ovl_banks[bank] = ("idle", self.cnt["sv"])

        def try_ovl_group():
            if not ovl_q:
                return False
            bank = wtok = None
            for b_ in (0, 1, 2):
                st_ = ovl_banks.get(b_)
                if st_ is not None and st_[0] == "idle":
                    bank, wtok = b_, st_[1]
                    break
            if bank is None:
                return False
            for qi_ in range(len(ovl_q)):
                g = ovl_q[qi_]
                if g[0] == "qk":
                    if g[2] not in ovl_ready:
                        continue
                elif g[1] not in vt_tok:
                    continue
                ovl_q.pop(qi_)
                emit_ovl_group(g, bank, wtok)
                return True
            return False

        for i in range(NU + 1):
            age_av()
            flush_av()

            if not try_ovl_group() and not ovl_started[0] and KW_ENABLE:
                if self._kw_pending is not None:
                    kb, kev = self._kw_pending
                    self._kw_pending = None

                    def tkw0(tensor, kb=kb, kev=kev):
                        mk.w(tensor, sm, "sv", kev)
                        i2 = tensor.matmul(mk.psB[:, ts(kb, 512)], mk.zkw[:, :],
                                           mk.maskb[:, 0:512], start=True, stop=True)
                        mk.inc("st", i2)
                    self.T(tkw0)
                    self._kw_bank = kb
                elif self._kw_bank is not None:
                    def tkw(tensor, kb=self._kw_bank):
                        for _ in range(3):
                            tensor.matmul(mk.psB[:, ts(kb, 512)], mk.zkw[:, :],
                                          mk.maskb[:, 0:512], start=True, stop=True)
                    self.T(tkw)

            if i >= 1:
                def t1(tensor, i=i):
                    if exp_tok.get((1, i - 2)):
                        mk.w(tensor, sm, "ss", exp_tok[(1, i - 2)])
                    mk.inc("st", emit_scores_pair(tensor, [(1, i - 1)]))
                self.T(t1)
                sc_tok[(1, i - 1)] = self.cnt["st"]
            if i < NU:
                def t0(tensor, i=i):
                    if exp_tok.get((0, i - 1)):
                        mk.w(tensor, sm, "ss", exp_tok[(0, i - 1)])
                    if i == 0:
                        mk.w(tensor, sm, "sv", qkv_v)
                        mk.w(tensor, sm, "sv", qkv_s)
                        if zinit_tok:
                            mk.w(tensor, sm, "sv", zinit_tok)
                    mk.inc("st", emit_scores_pair(tensor, [(0, i)]))
                self.T(t0)
                sc_tok[(0, i)] = self.cnt["st"]

            for hh, ui in ((1, i - 1), (0, i)):
                if 0 <= ui < NU and UL[ui][4]:
                    def v(vector, hh=hh, ui=ui):
                        kt = UL[ui][0]
                        mk.w(vector, sm, "st", sc_tok[(hh, ui)])
                        dl = 128 * (kt % 4)
                        i2 = vector.reduce_max(mk.negc[:, hh:hh + 1],
                                               sets[hh][:, dl:dl + 128],
                                               axis=AX, negate=True)
                        mk.inc("sv", i2)
                    self.V(v)
                    if not hasattr(self, "_negc_tok"):
                        self._negc_tok = {}
                    self._negc_tok[(hh, UL[ui][0])] = self.cnt["sv"]

            for hh, ui in ((1, i - 1), (0, i)):
                if not (0 <= ui < NU):
                    continue
                kt, u, qs, wdt, first, last = UL[ui]
                gkt = gkt0 + kt

                if first:
                    flush_av(need_gkt=gkt - 2)

                def sx(scalar, hh=hh, ui=ui, kt=kt, u=u, qs=qs, wdt=wdt,
                       first=first, gkt=gkt):
                    mk.w(scalar, sm, "st", sc_tok[(hh, ui)])
                    mk.w(scalar, sm, "sv", self._negc_tok[(hh, kt)])
                    if first and self.av_tok.get(gkt - 2):
                        mk.w(scalar, sm, "st", self.av_tok[gkt - 2])
                    abf = mk.abuf[2 * (gkt % 2) + hh]
                    j0 = kt >> 2
                    i2 = scalar.activation(
                        out=abf[:, qs - 512 * j0: qs - 512 * j0 + wdt],
                        in_=sets[hh][:, 0:wdt], func=AF.Exp,
                        bias=mk.negc[:, hh:hh + 1], scale=1.0,
                        accum_out=mk.ztm[:, hh * 4 + u: hh * 4 + u + 1])
                    mk.inc("ss", i2)
                self.Sc(sx)
                exp_tok[(hh, ui)] = self.cnt["ss"]

                if last:
                    flush_av(need_gkt=gkt - 2)

                    def v2(vector, hh=hh, kt=kt, u=u, gkt=gkt):
                        mk.w(vector, sm, "ss", exp_tok[(hh, ui)] if False
                             else self.cnt["ss"])
                        if self.av_tok.get(gkt - 2):
                            mk.w(vector, sm, "st", self.av_tok[gkt - 2])
                        if u > 0:
                            vector.tensor_tensor(
                                mk.zts[:, hh:hh + 1],
                                mk.ztm[:, hh * 4: hh * 4 + 1],
                                mk.ztm[:, hh * 4 + 1: hh * 4 + 2], op=ADD)
                            vector.drain()
                            vector.reciprocal(mk.rz[:, hh:hh + 1],
                                              mk.zts[:, hh:hh + 1])
                        else:
                            vector.reciprocal(mk.rz[:, hh:hh + 1],
                                              mk.ztm[:, hh * 4: hh * 4 + 1])
                        vector.drain()
                        po = 64 * hh
                        i2 = vector.tensor_scalar(
                            mk.vt[:, ts(2 * (gkt % 2) + hh, 64)],
                            mk.vp[:, kt * 128 + po: kt * 128 + po + 64],
                            mk.rz[:, hh:hh + 1], None, op0=MULT)
                        mk.inc("sv", i2)
                    self.V(v2)
                    if hh == 1:
                        vt_tok[kt] = self.cnt["sv"]

                        def t2(tensor, kt=kt, gkt=gkt):
                            mk.w(tensor, sm, "sv", vt_tok[kt])
                            j0 = kt >> 2
                            i2 = None
                            for j in range(j0, 4):
                                for h2 in range(2):
                                    i2 = tensor.matmul(
                                        mk.psB[64 * h2:64 * h2 + 64, ts(j, 512)],
                                        mk.vt[:, ts(2 * (gkt % 2) + h2, 64)],
                                        mk.abuf[2 * (gkt % 2) + h2][:, ts(j - j0, 512)],
                                        start=(kt == 0 and h2 == 0),
                                        stop=(kt == 4 * j + 3 and h2 == 1),
                                        tile_position=(0, 64 * h2),
                                        skip_group_check=True)
                            mk.inc("st", i2)

                        def mkpend(t2=t2, gkt=gkt, kt=kt):
                            def run():
                                self.T(t2)
                                self.av_tok[gkt] = self.cnt["st"]
                                if kt % 4 == 3 and EARLY_EVAC:
                                    b = kt >> 2

                                    def ve(vector, b=b, gkt=gkt):
                                        mk.w(vector, sm, "st", self.av_tok[gkt])
                                        vector.tensor_copy(
                                            mk.zbuf[:, p * S + 512 * b:
                                                    p * S + 512 * b + 512],
                                            mk.psB[:, ts(b, 512)])
                                        vector.drain()
                                        i3 = vector.memset(
                                            mk.psB[64:128, ts(b, 512)], 0.0)
                                        mk.inc("sv", i3)
                                    self.V(ve)
                                    if b == 3:
                                        self.zevac_tok[(sq, p)] = self.cnt["sv"]
                                    else:
                                        self._kw_pending = (b, self.cnt["sv"])
                                        ovl_ready[b] = self.cnt["sv"]
                                        ovl_banks[b] = ("idle", self.cnt["sv"])
                            return run
                        pend_q.append((gkt, 0, mkpend()))

        flush_av(max_age=0)
        self.gkt = gkt0 + NKT
        for b_ in sorted(ovl_banks):
            tok_ = ovl_banks[b_][1]

            def tz(tensor, b_=b_, tok_=tok_):
                mk.w(tensor, sm, "sv", tok_)
                i2 = tensor.matmul(mk.psB[:, ts(b_, 512)], mk.zkw[:, :],
                                   mk.maskb[:, 0:512], start=True, stop=True)
                mk.inc("st", i2)
            self.T(tz)
        if p + 1 < NPAIR:
            self.ovl_done = {"pair": (sq, p + 1), "qk": ovl_done_qk,
                             "v": ovl_done_v}
        if not EARLY_EVAC:
            def v3(vector, p=p):
                mk.w(vector, sm, "st", self.av_tok[self.gkt - 1])
                for j in range(4):
                    vector.tensor_copy(mk.zbuf[:, p * S + 512 * j: p * S + 512 * j + 512],
                                       mk.psB[:, ts(j, 512)])
                vector.drain()
                i = vector.memset(mk.psB[64:128, :], 0.0)
                mk.inc("sv", i)
            self.V(v3)
            self.zevac_tok[(sq, p)] = self.cnt["sv"]

    def emit_wo_ln1(self, sq):
        mk, sm = self, self.sems
        zdone = self.zevac_tok[(sq, NPAIR - 1)]
        att_done = self.av_tok[self.gkt - 1]

        def dmas(sync, incf):
            for dc in range(NDC):
                incf(sync.dma_start(mk.wot[:, ts(dc, D)], mk.s_woT[ts(dc, 128), :]))
        wref = self.sync_group(dmas, extra_waits=[("st", att_done), ("sv", zdone)])

        ev_tok = {}
        pend_apply = None
        for jw in range(NW):
            for oc in range(NDC):
                bank = 2 + (oc % 2)

                def t(tensor, jw=jw, oc=oc, bank=bank):
                    if jw == 0 and oc == 0:
                        mk.w(tensor, sm, wref["sem"], wref["tok"])
                        mk.w(tensor, sm, "sv", zdone)
                    if (jw, oc) > (0, 1) or (jw == 0 and oc > 1):
                        prev = (jw, oc - 2) if oc >= 2 else (jw - 1, NDC - 2 + oc)
                        mk.w(tensor, sm, "sv", ev_tok[prev])
                    for zc in range(NDC):
                        i = tensor.matmul(mk.psA[:, ts(bank, 512)],
                                          mk.wot[:, zc * D + oc * 128: zc * D + oc * 128 + 128],
                                          mk.zbuf[:, ts(zc, S)][:, ts(jw, 512)],
                                          start=(zc == 0), stop=(zc == NDC - 1))
                    mk.inc("st", i)
                self.T(t)
                mmt = self.cnt["st"]

                def v(vector, jw=jw, oc=oc, bank=bank, mmt=mmt):
                    mk.w(vector, sm, "st", mmt)
                    i = vector.scalar_tensor_tensor(
                        out=mk.xfr(oc, jw),
                        in0=mk.psA[:, ts(bank, 512)],
                        scalar=mk.bo_s[:, oc:oc + 1],
                        in1=mk.xf(oc, jw),
                        op0=ADD, op1=ADD)
                    mk.inc("sv", i)
                self.V(v)
                ev_tok[(jw, oc)] = self.cnt["sv"]
            if pend_apply is not None:
                pend_apply()
            pend_apply = self.emit_ln(sq, jw, last_writer=ev_tok[(jw, NDC - 1)],
                                      eps=EPS_LN, to_out=False, phase="ln1", jw=jw)
        pend_apply()
        self.ref_consumed(wref, "st", self.cnt["st"])

    def emit_ln(self, sq, w, last_writer, eps, to_out, phase, jw):
        mk, sm = self, self.sems
        sq_tok = {}
        mm_tok = {}
        for dc in range(NDC):
            slot = 0

            def s(scalar, dc=dc, slot=slot):
                mk.w(scalar, sm, "sv", last_writer)
                if dc >= 1:
                    mk.w(scalar, sm, "st", mm_tok[dc - 1])
                i = scalar.activation(out=mk.sq[slot][:, 0:512],
                                      in_=mk.yf(dc, jw),
                                      func=AF.Square, bias=0.0, scale=1.0)
                mk.inc("ss", i)
            self.Sc(s)
            sq_tok[dc] = self.cnt["ss"]

            def t(tensor, dc=dc, slot=slot):
                mk.w(tensor, sm, "ss", sq_tok[dc])
                if dc == 0:
                    mk.w(tensor, sm, "sv", last_writer)
                tensor.matmul(mk.psA[:, 0:512], mk.ones_r[:, :],
                              mk.xfr(dc, jw),
                              start=(dc == 0), stop=(dc == NDC - 1))
                i = tensor.matmul(mk.psA[:, 512:1024], mk.ones_r[:, :],
                                  mk.sq[slot][:, 0:512],
                                  start=(dc == 0), stop=(dc == NDC - 1))
                mk.inc("st", i)
            self.T(t)
            mm_tok[dc] = self.cnt["st"]

        slot = jw % 2

        def v(vector, slot=slot):
            mk.w(vector, sm, "st", mm_tok[NDC - 1])
            if hasattr(self, "ln_apply_tok") and self.ln_apply_tok.get(slot):
                mk.w(vector, sm, "sv", self.ln_apply_tok[slot])
            vector.tensor_scalar(mk.lnm[slot][:, 0:512], mk.psA[:, 0:512], 1.0 / D,
                                 None, op0=MULT)
            vector.tensor_scalar(mk.lnr[slot][:, 0:512], mk.psA[:, 512:1024], 1.0 / D,
                                 None, op0=MULT)
            vector.drain()
            vector.tensor_tensor(mk.tmp1[:, 0:512], mk.lnm[slot][:, 0:512],
                                 mk.lnm[slot][:, 0:512], op=MULT)
            vector.drain()
            i = vector.tensor_tensor(mk.lnr[slot][:, 0:512], mk.lnr[slot][:, 0:512],
                                     mk.tmp1[:, 0:512], op=SUB)
            mk.inc("sv", i)
        self.V(v)
        var_tok = self.cnt["sv"]

        def s2(scalar, slot=slot):
            mk.w(scalar, sm, "sv", var_tok)
            i = scalar.activation(out=mk.tmp1[:, 0:512], in_=mk.lnr[slot][:, 0:512],
                                  func=AF.Sqrt,
                                  bias=(mk.eps_ln if eps == EPS_LN else mk.eps_rms)[:, :],
                                  scale=1.0)
            mk.inc("ss", i)
        self.Sc(s2)
        sd_tok = self.cnt["ss"]

        def v2(vector, slot=slot):
            mk.w(vector, sm, "ss", sd_tok)
            i = vector.reciprocal(mk.lnr[slot][:, 0:512], mk.tmp1[:, 0:512])
            mk.inc("sv", i)
        self.V(v2)
        rstd_tok = self.cnt["sv"]

        def apply_emit():
            ap_tok = {}
            for dc in range(NDC):
                def v3(vector, dc=dc, slot=slot):
                    if dc == 0:
                        mk.w(vector, sm, "sv", rstd_tok)
                    vector.tensor_tensor(mk.tmp1[:, 0:512],
                                         mk.yf(dc, jw),
                                         mk.lnm[slot][:, 0:512], op=SUB)
                    vector.drain()
                    out_ap = mk.xfr(dc, jw)
                    i = vector.tensor_tensor(out_ap, mk.tmp1[:, 0:512],
                                             mk.lnr[slot][:, 0:512], op=MULT)
                    vector.drain()
                    mk.inc("sv", i)
                self.V(v3)
                ap_tok[dc] = self.cnt["sv"]
            if not hasattr(self, "ln_apply_tok"):
                self.ln_apply_tok = {}
            self.ln_apply_tok[slot] = self.cnt["sv"]
            if phase == "ln1":
                if not hasattr(self, "ln1_tok"):
                    self.ln1_tok = {}
                self.ln1_tok[(sq, jw)] = self.cnt["sv"]
            else:
                for dc in range(NDC):
                    def g(gp, dc=dc):
                        mk.w(gp, sm, "sv", ap_tok[dc])
                        i = gp.dma_start(mk.outT[sq, ts(dc, 128), ts(jw, 512)],
                                         mk.yf(dc, jw))
                        mk.inc("so", i, 16)
                    self.G(g)
        return apply_emit

    def emit_ffn(self, sq):
        mk, sm = self, self.sems
        ln1_all = self.ln1_tok[(sq, NW - 1)]
        qkv_last = self.qkv_mm_done[(sq, NPAIR - 1)]

        extra1 = [("st", qkv_last)]
        if sq == 0:
            extra1.append(("sgs", self.comb_store))
            extra1.append(("sgs", self.proj_store))

        def dmas1(sync, incf):
            for dc in range(4):
                incf(sync.dma_start(mk.wcomb[:, ts(dc, FF)], mk.s_comb[ts(dc, 128), :]))
        cref1 = self.sync_group(dmas1, extra_waits=extra1)

        def dmas2(sync, incf):
            for dc in range(4, NDC):
                incf(sync.dma_start(mk.wcomb[:, ts(dc, FF)], mk.s_comb[ts(dc, 128), :]))
        cref2 = self.sync_group(
            dmas2, extra_waits=[("sv", self.zevac_tok[(sq, NPAIR - 1)]),
                                ("st", self.cnt["st"])])

        relu_tok = {}
        r2_tok = {}
        pend_ln2 = None
        self.x2b_tok = getattr(self, "x2b_tok", {})
        for w in range(NW):
            f1_first = {}

            def sc_cast(scalar, w=w):
                mk.w(scalar, sm, "sv", ln1_all)
                if f1_first.get("prev") is not None:
                    mk.w(scalar, sm, "st", f1_first["prev"])
                i = None
                for dc in range(NDC):
                    i = scalar.activation(out=mk.x2bw[:, ts(dc, 512)],
                                          in_=mk.yf(dc, w), func=AF.Copy,
                                          bias=0.0, scale=1.0)
                mk.inc("ss", i)
            if (w > 0 or sq > 0) and hasattr(self, "f1_done") and self.f1_done:
                f1_first["prev"] = self.f1_done
            self.Sc(sc_cast)
            self.x2b_tok[(sq, w)] = self.cnt["ss"]

            kwwait = (self.y2_tok[(w - 1, NDC - 1)] if w > 0
                      else self.zevac_tok[(sq, NPAIR - 1)])

            def tkwa(tensor, kwwait=kwwait):
                mk.w(tensor, sm, "sv", kwwait)
                for k_ in range(10):
                    tensor.matmul(mk.psB[:, ts(k_ % 4, 512)], mk.zkw[:, :],
                                  mk.maskb[:, 0:512], start=True, stop=True)
            self.T(tkwa)

            for hc in range(32):
                bank = hc % 4

                svsnap = self.cnt["sv"]

                def t(tensor, w=w, hc=hc, bank=bank, svsnap=svsnap):
                    if w == 0 and hc == 0:
                        mk.w(tensor, sm, cref1["sem"], cref1["tok"])
                        mk.w(tensor, sm, cref2["sem"], cref2["tok"])
                        mk.w(tensor, sm, "sv", ln1_all)
                    if w > 0 and hc == 0:
                        mk.w(tensor, sm, "sv", svsnap)
                    if hc == 0:
                        mk.w(tensor, sm, "ss", self.x2b_tok[(sq, w)])
                    if (w, hc) >= (0, 4):
                        prev = (w, hc - 4) if hc >= 4 else (w - 1, 28 + hc)
                        if prev in relu_tok:
                            mk.w(tensor, sm, "ss", relu_tok[prev])
                    for dc in range(NDC):
                        i = tensor.matmul(mk.psA[:, ts(bank, 512)],
                                          mk.wcomb[:, dc * FF + hc * 128: dc * FF + hc * 128 + 128],
                                          mk.x2bw[:, ts(dc, 512)],
                                          start=(dc == 0), stop=(dc == NDC - 1))
                    mk.inc("st", i)
                self.T(t)
                mmt = self.cnt["st"]

                def s(scalar, w=w, hc=hc, bank=bank, mmt=mmt):
                    mk.w(scalar, sm, "st", mmt)
                    if hc >= 2 or w > 0:
                        prev = (w, hc - 2) if hc >= 2 else (w - 1, 30 + hc)
                        if prev in r2_tok:
                            mk.w(scalar, sm, "sv", r2_tok[prev])
                    i = scalar.activation(out=mk.rl[hc % 2][:, 0:512],
                                          in_=mk.psA[:, ts(bank, 512)],
                                          func=AF.Relu, bias=0.0, scale=1.0)
                    mk.inc("ss", i)
                self.Sc(s)
                relu_tok[(w, hc)] = self.cnt["ss"]

                def v(vector, w=w, hc=hc, mmt=mmt):
                    mk.w(vector, sm, "ss", relu_tok[(w, hc)])
                    i = vector.tensor_tensor(mk.r2[:, ts(hc, 512)],
                                             mk.rl[hc % 2][:, 0:512],
                                             mk.rl[hc % 2][:, 0:512], op=MULT)
                    mk.inc("sv", i)
                self.V(v)
                r2_tok[(w, hc)] = self.cnt["sv"]

            self.f1_done = self.cnt["st"]

            def tkwb(tensor):
                for k_ in range(16):
                    tensor.matmul(mk.psB[:, ts(k_ % 4, 512)], mk.zkw[:, :],
                                  mk.maskb[:, 0:512], start=True, stop=True)
            self.T(tkwb)

            pj_mm = {}
            for hcp in range(16):
                hc0 = hcp * 2
                pslot = hcp % 2
                if hcp >= 2:
                    extra2 = [("st", pj_mm[hc0 - 3])]
                elif hcp == 0 and w > 0:
                    extra2 = [("st", self.pj_mm29[w - 1])]
                elif w > 0:
                    extra2 = [("st", self.pj_done[w - 1])]
                else:
                    extra2 = [("st", self.cnt["st"])]
                pref = self.sync_group(
                    lambda sync, incf, hc0=hc0, pslot=pslot: incf(
                        sync.dma_start(
                            mk.wpr0[:, pslot * 2 * D:(pslot + 1) * 2 * D]
                            .rearrange("b (a c) -> b a c", a=2),
                            mk.s_proj.rearrange("(a b) c -> b a c", b=128)
                            [:, hc0:hc0 + 2, :])),
                    extra_waits=extra2)
                ldn, wt = pref["sem"], pref["tok"]

                for hc in (hc0, hc0 + 1):
                    slot = 2 * pslot + (hc % 2)

                    def t2(tensor, w=w, hc=hc, slot=slot, ldn=ldn, wt=wt):
                        if hc % 2 == 0:
                            mk.w(tensor, sm, ldn, wt)
                        mk.w(tensor, sm, "sv", r2_tok[(w, hc)])
                        if hc == 0:
                            mk.w(tensor, sm, "ss", relu_tok[(w, 31)])
                            if w > 0:
                                mk.w(tensor, sm, "sv", self.y2_tok[(w - 1, NDC - 1)])
                        for oc in range(NDC):
                            ps = mk.psA if oc < 4 else mk.psB
                            i = tensor.matmul(ps[:, ts(oc % 4, 512)],
                                              mk.wpr0[:, slot * D + oc * 128:
                                                      slot * D + oc * 128 + 128],
                                              mk.r2[:, ts(hc, 512)],
                                              start=(hc == 0), stop=(hc == 31))
                        mk.inc("st", i)
                    self.T(t2)
                    pj_mm[hc] = self.cnt["st"]
                self.ref_consumed(pref, "st", pj_mm[hc0 + 1])
            self.pj_done = getattr(self, "pj_done", {})
            self.pj_done[w] = self.cnt["st"]
            self.pj_mm29 = getattr(self, "pj_mm29", {})
            self.pj_mm29[w] = pj_mm[29]

            self.y2_tok = getattr(self, "y2_tok", {})
            for oc in range(NDC):
                def v2(vector, w=w, oc=oc):
                    mk.w(vector, sm, "st", self.pj_done[w])
                    ps = mk.psA if oc < 4 else mk.psB
                    i = vector.tensor_tensor(
                        mk.xfr(oc, w),
                        ps[:, ts(oc % 4, 512)],
                        mk.yf(oc, w), op=ADD)
                    mk.inc("sv", i)
                self.V(v2)
                self.y2_tok[(w, oc)] = self.cnt["sv"]

            if pend_ln2 is not None:
                pend_ln2()
            pend_ln2 = self.emit_ln(sq, w, last_writer=self.y2_tok[(w, NDC - 1)],
                                    eps=EPS_LN, to_out=True, phase="ln2", jw=w)
            if w == 0:
                self.ref_consumed(cref1, "st", self.cnt["st"])
                self.ref_consumed(cref2, "st", self.cnt["st"])
        return pend_ln2


    def _flush_ln2(self, pend):
        if pend is not None:
            pend()

    def emit_final(self):
        mk, sm = self, self.sems
        total = self.cnt["so"]

        def g(gp):
            gp.wait_ge(sm["so"], total)
        self.G(g)


def _prep_inputs(inputs):
    x = np.asarray(inputs["L0_attn_x"], dtype=np.float32)
    W_qkv = np.asarray(inputs["L0_attn_W_qkv"], dtype=np.float32)
    b_qkv = np.asarray(inputs["L0_attn_b_qkv"], dtype=np.float32)
    W_o = np.asarray(inputs["L0_attn_W_o"], dtype=np.float32)
    b_o = np.asarray(inputs["L0_attn_b_o"], dtype=np.float32)
    fb = np.asarray(inputs["L0_ffn_fast_b"], dtype=np.float32)
    w_fc = np.asarray(inputs["L0_ffn_w_fc"], dtype=np.float32)
    w_proj = np.asarray(inputs["L0_ffn_w_proj"], dtype=np.float32)
    u = np.asarray(inputs["L0_ffn_u"], dtype=np.float32)
    v = np.asarray(inputs["L0_ffn_v"], dtype=np.float32)
    gate = float(np.asarray(inputs["L0_ffn_gate"]))

    hh = np.arange(H).repeat(DH)
    dd = np.tile(np.arange(DH), H)
    idx = np.concatenate([hh * 192 + dd * 3 + s for s in range(3)])
    Wp = W_qkv[idx]
    bp = b_qkv[idx]
    shared = {
        "wqkvT": np.ascontiguousarray(Wp.T),
        "bqkv": np.ascontiguousarray(bp),
        "woT": np.ascontiguousarray(W_o.T),
        "bo": np.ascontiguousarray(b_o),
        "wfcT": np.ascontiguousarray(w_fc.T),
        "wprojT": np.ascontiguousarray(w_proj.T),
        "wfcN": np.ascontiguousarray(w_fc),
        "wprojN": np.ascontiguousarray(w_proj),
        "uT": np.ascontiguousarray(u.T),
        "v16": np.ascontiguousarray(v),
        "fbT": np.ascontiguousarray(fb.T),
    }
    kl = np.arange(128)[:, None]
    ql = np.arange(512)[None, :]
    m = np.concatenate(
        [np.where(ql >= 128 * r + kl, 0.0, -1.0e30).astype(np.float32)
         for r in range(4)], axis=1)
    shared["masks"] = np.ascontiguousarray(m)
    shared["ident"] = np.eye(128, dtype=np.float32)

    in_maps = []
    for c in range(8):
        im = dict(shared)
        xc = x[2 * c: 2 * c + 2]
        im["xT"] = np.ascontiguousarray(xc.transpose(0, 2, 1))
        in_maps.append(im)
    return in_maps, gate


def kernel(**inputs):
    in_maps, gate = _prep_inputs(inputs)
    mk = MK(gate)
    nc = mk.build()
    res = run_bass_kernel_spmd(nc, in_maps, core_ids=list(range(8)))
    outs = []
    for c in range(8):
        o = res.results[c]["outT"]
        outs.append(np.ascontiguousarray(o.transpose(0, 2, 1)))
    return np.concatenate(outs, axis=0)


# version 24 (final)
# speedup vs baseline: 1.0234x; 1.0234x over previous
import numpy as np
from contextlib import ExitStack

import concourse.bass as bass
import concourse.mybir as mybir
from concourse.alu_op_type import AluOpType
from concourse.bass_utils import run_bass_kernel_spmd

F32 = mybir.dt.float32
F32R = mybir.dt.float32r
BF16 = mybir.dt.bfloat16
AF = mybir.ActivationFunctionType
AX = mybir.AxisListType.X
ADD = AluOpType.add
SUB = AluOpType.subtract
MULT = AluOpType.mult
MAXOP = AluOpType.max
ISGT = AluOpType.is_gt
ABSMAX = AluOpType.abs_max

D = 1024
S = 2048
NSEQ = 2
H = 16
DH = 64
FF = 4096
NDC = D // 128
NKT = S // 128
NW = S // 512
NPAIR = H // 2
EPS_RMS = float(np.float32(1.1920929e-07))
EPS_LN = 1e-5

KW_ENABLE = True
EARLY_EVAC = True

TOP = 229344
XBUF_OFF = TOP - 65536
R = XBUF_OFF - 109056
A_NORMED = R + 0
A_ZBUF = R + 32768
A_QP = R + 65536
A_KP = R + 69632
A_VP = R + 73728
A_WQKV = R + 77824
A_ABUF = R + 90112
A_VT = R + 98304
A_SQB = R + 98560
A_WOT = R + 77824
F_WCOMB = R + 0
F_R2 = R + 65536
F_WPROJ = R + 98304
ST_INVR = R + 102400
ST_B = R + 104448
ST_SQ = R + 106496
P_WLOAD = R + 0
P_AW2 = R + 32768
P_SCF = R + 49152
P_SCB = R + 65536
P_WEFF = R + 73728
P_UTB = R + 82944
P_T1B = R + 91136
P_V16B = R + 93184
P_FBTB = R + 95232
P_LRB = R + 95296


def ts(i, n):
    return slice(i * n, (i + 1) * n)


class MK:
    def __init__(self, gate):
        self.gate = float(gate)
        nc = bass.Bass("TRN2", target_bir_lowering=False)
        self.nc = nc
        dt = nc.dram_tensor
        self.xT = dt("xT", [NSEQ, D, S], F32, kind="ExternalInput")
        self.wqkvT = dt("wqkvT", [D, 3 * D], F32, kind="ExternalInput")
        self.bqkv = dt("bqkv", [3 * D], F32, kind="ExternalInput")
        self.woT_d = dt("woT", [D, D], F32, kind="ExternalInput")
        self.bo_d = dt("bo", [D], F32, kind="ExternalInput")
        self.wfcT = dt("wfcT", [D, FF], F32, kind="ExternalInput")
        self.wprojT = dt("wprojT", [FF, D], F32, kind="ExternalInput")
        self.uT_d = dt("uT", [16, FF], F32, kind="ExternalInput")
        self.v16_d = dt("v16", [16, D], F32, kind="ExternalInput")
        self.fbT_d = dt("fbT", [16, 16], F32, kind="ExternalInput")
        self.wfcN = dt("wfcN", [FF, D], F32, kind="ExternalInput")
        self.wprojN = dt("wprojN", [D, FF], F32, kind="ExternalInput")
        self.masks_d = dt("masks", [128, 4 * 512], F32, kind="ExternalInput")
        self.ident_d = dt("ident", [128, 128], F32, kind="ExternalInput")
        self.outT = dt("outT", [NSEQ, D, S], F32, kind="ExternalOutput")
        self.s_qkvT = dt("s_qkvT", [D, 3 * D], BF16, kind="Internal")
        self.s_woT = dt("s_woT", [D, D], BF16, kind="Internal")
        self.s_comb = dt("s_comb", [D, FF], BF16, kind="Internal")
        self.s_proj = dt("s_proj", [FF, D], BF16, kind="Internal")
        self.s_scale = dt("s_scale", [FF], F32, kind="Internal")

        at = nc.alloc_sbuf_tensor_at
        self.xb = at("xb", [128, NDC * S], F32, offset=XBUF_OFF)
        self.xr = at("xr", [128, NDC * S], F32R, offset=XBUF_OFF)
        self.normed = at("normed", [128, NDC * S], BF16, offset=A_NORMED)
        self.zbuf = at("zbuf", [128, NDC * S], BF16, offset=A_ZBUF)
        self.qp = at("qp", [128, S], BF16, offset=A_QP)
        self.kp = at("kp", [128, S], BF16, offset=A_KP)
        self.vp = at("vp", [128, S], BF16, offset=A_VP)
        self.wqkv = [at("wqkv%d" % i, [128, 3 * D], BF16, offset=A_WQKV + i * 6144)
                     for i in range(2)]
        self.abuf = [at("abuf%d" % i, [128, S], BF16, offset=off)
                     for i, off in enumerate((A_ABUF, A_ABUF + 4096,
                                              A_VT + 512, A_VT + 512 + 4096))]
        self.vt = at("vt", [128, 256], BF16, offset=A_VT)
        self.wot = at("wot", [128, NDC * D], BF16, offset=A_WOT)
        self.wcomb = at("wcomb", [128, NDC * FF], BF16, offset=F_WCOMB)
        self.r2 = at("r2", [128, 32 * 512], BF16, offset=F_R2)
        self.wpr0 = at("wpr0", [128, 4 * D], BF16, offset=F_WPROJ)
        self.rl = [at("rla", [128, 512], BF16, offset=ST_SQ),
                   at("rlb", [128, 512], BF16, offset=ST_SQ + 1024)]
        self.invr = at("invr", [128, 512], F32, offset=ST_INVR)
        self.stb = at("stb", [128, 512], F32, offset=ST_B)
        self.sq = [at("sqa", [128, 512], F32R, offset=ST_SQ)]
        self.wload = [at("wload%d" % i, [128, 4096], F32, offset=P_WLOAD + i * 16384)
                      for i in range(2)]
        self.aw2 = at("aw2", [128, 4096], F32R, offset=P_AW2)
        self.aw2f = at("aw2f", [128, 4096], F32, offset=P_AW2)
        self.scf = at("scf", [128, 4096], F32, offset=P_SCF)
        self.scb = at("scb", [128, 4096], BF16, offset=P_SCB)
        self.weff = at("weff", [128, 4096], BF16, offset=P_WEFF)
        self.utb = at("utb", [16, 4096], BF16, offset=P_UTB)
        self.t1b = at("t1b", [16, 1024], BF16, offset=P_T1B)
        self.v16b = at("v16b", [16, 1024], BF16, offset=P_V16B)
        self.fbtb = at("fbtb", [16, 16], BF16, offset=P_FBTB)
        self.lrb = at("lrb", [128, 4096], BF16, offset=P_LRB)

        self.es = ExitStack()
        sb = lambda n, sh, d: self.es.enter_context(nc.sbuf_tensor(n, sh, d))
        self.maskb = sb("maskb", [128, 4 * 512], BF16)
        self.identb = sb("identb", [128, 128], BF16)
        self.ones_r = sb("ones_rt", [128, 128], F32R)
        self.bqkv_s = sb("bqkv_s", [128, 24], F32)
        self.bo_s = sb("bo_s", [128, 8], F32)
        self.bvb = sb("bvb", [128, D], F32)
        self.negc = sb("negc", [128, 2], F32)
        self.zta = sb("zta", [128, 2], F32)
        self.ztb = sb("ztb", [128, 2], F32)
        self.zts = sb("zts", [128, 2], F32)
        self.ztm = sb("ztm", [128, 8], F32)
        self.rz = sb("rz", [128, 2], F32)
        self.eps_rms = sb("eps_rms", [128, 1], F32)
        self.eps_ln = sb("eps_ln", [128, 1], F32)
        self.surv = sb("surv", [128, 1024], F32)
        self.errs = sb("errs", [128, 1024], F32)
        self.scaleN = sb("scaleN", [128, 32], F32)
        self.x2bw = sb("x2bw", [128, 4096], BF16)
        self.lnm = [sb("lnm%d" % i, [128, 512], F32) for i in range(2)]
        self.lnr = [sb("lnr%d" % i, [128, 512], F32) for i in range(2)]
        self.tmp1 = sb("tmp1", [128, 512], F32)
        self.ldst = sb("ldst", [128, 512], F32)
        self.zkw = at("zkw", [128, 128], BF16, offset=ST_SQ + 2048)

        self.psA = self.es.enter_context(nc.psum_tensor("psA", [128, 2048], F32))
        self.psB = self.es.enter_context(nc.psum_tensor("psB", [128, 2048], F32))

        self.sems = {}
        self.cnt = {}
        for name in ["st", "sv", "ss", "swA", "swB", "swC", "swD", "sgx", "sgs", "so"]:
            self.sems[name] = self.es.enter_context(nc.semaphore(name))
            self.cnt[name] = 0
        self.block = self.es.enter_context(nc.Block())

    class _MockInstr:
        __slots__ = ("ops",)

        def __init__(self, ops):
            self.ops = ops

        def then_inc(self, sem, k=1):
            self.ops.append(("__then_inc__", (sem, k), {}))
            return self


    class _Rec:

        def __init__(self, ops):
            self._ops = ops

        def __getattr__(self, name):
            ops = self._ops

            def call(*a, **kw):
                ops.append((name, a, kw))
                return MK._MockInstr(ops)
            return call

    def xf(self, dc, w):
        return self.xb[:, ts(dc, S)][:, ts(w, 512)]

    def xfr(self, dc, w):
        return self.xr[:, ts(dc, S)][:, ts(w, 512)]

    def yf(self, dc, w):
        return self.xr[:, ts(dc, S)][:, ts(w, 512)].bitcast(F32)

    def _rec(self, key, f):
        if not hasattr(self, "_oplists"):
            self._oplists = {k: [] for k in ("tensor", "vector", "scalar", "sync", "gpsimd")}
        f(MK._Rec(self._oplists[key]))

    def T(self, f):
        self._rec("tensor", f)

    def V(self, f):
        self._rec("vector", f)

    def Sc(self, f):
        self._rec("scalar", f)

    def Sy(self, f):
        self._rec("sync", f)

    def G(self, f):
        self._rec("gpsimd", f)

    def _replay_all(self):
        def replay(ops, eng):
            last = None
            for (name, a, kw) in ops:
                if name == "__then_inc__":
                    last.then_inc(*a)
                else:
                    last = getattr(eng, name)(*a, **kw)
        self.block.tensor(lambda e: replay(self._oplists["tensor"], e))
        self.block.vector(lambda e: replay(self._oplists["vector"], e))
        self.block.scalar(lambda e: replay(self._oplists["scalar"], e))
        self.block.sync(lambda e: replay(self._oplists["sync"], e))
        self.block.gpsimd(lambda e: replay(self._oplists["gpsimd"], e))

    def sync_group(self, emit_fn, extra_waits=()):
        if not hasattr(self, "_ring"):
            self._ring = ["swA", "swB", "swC", "swD"]
            self._ring_i = 0
            self._ring_prev = {}
        sname = self._ring[self._ring_i % 4]
        self._ring_i += 1
        prev = self._ring_prev.get(sname)
        mk, sm = self, self.sems

        def sy(sync):
            if prev is not None:
                for (tn, tv) in prev["consumers"]:
                    mk.w(sync, sm, tn, tv)
            for (tn, tv) in extra_waits:
                mk.w(sync, sm, tn, tv)
            emit_fn(sync, lambda instr: mk.inc(sname, instr, 16))
        self.Sy(sy)
        ref = {"sem": sname, "tok": self.cnt[sname], "consumers": []}
        self._ring_prev[sname] = ref
        return ref

    @staticmethod
    def ref_consumed(ref, sem, tok):
        ref["consumers"].append((sem, tok))

    def inc(self, name, instr, k=1):
        instr.then_inc(self.sems[name], k)
        self.cnt[name] += k
        return self.cnt[name]

    @staticmethod
    def w(eng, sem_map, name, tok):
        if tok > 0:
            eng.wait_ge(sem_map[name], tok)

    def build(self):
        self.emit_consts()
        self.emit_phase0()
        for sq in range(NSEQ):
            self.emit_seq(sq)
        self.emit_final()
        self._replay_all()
        self.es.close()
        return self.nc

    def emit_consts(self):
        mk, sm = self, self.sems

        def dmas(sync, incf):
            incf(sync.dma_start(mk.wload[0][:, 0:2048], mk.masks_d[:, :]))
            incf(sync.dma_start(mk.ldst[:, 0:128], mk.ident_d[:, :]))
            incf(sync.dma_start(mk.bqkv_s[:].rearrange("p (a c) -> p a c", c=1),
                                mk.bqkv.rearrange("(a p) -> p a", p=128).unsqueeze(2),
                                allow_slow_non_contiguous=True))
            incf(sync.dma_start(mk.bo_s[:].rearrange("p (a c) -> p a c", c=1),
                                mk.bo_d.rearrange("(a p) -> p a", p=128).unsqueeze(2),
                                allow_slow_non_contiguous=True))
            incf(sync.dma_start(mk.bvb[0:1, :], mk.bqkv[2048:3072].unsqueeze(0)))
        ref = self.sync_group(dmas)

        def g(gp):
            mk.w(gp, sm, ref["sem"], ref["tok"])
            rows = 1
            while rows < 128:
                nxt = min(2 * rows, 128)
                i = gp.dma_start(mk.bvb[rows:nxt, :], mk.bvb[0:rows, :][0:nxt - rows, :])
                tok = mk.inc("sgx", i, 16)
                gp.wait_ge(sm["sgx"], tok)
                rows = nxt
            for dc in range(NDC):
                i = gp.dma_start(mk.xb[:, ts(dc, S)], mk.xT[0, ts(dc, 128), :])
                mk.inc("sgx", i, 16)
        self.G(g)
        self.xload_tok = [self.cnt["sgx"], None]
        self.ref_consumed(ref, "sgx", self.cnt["sgx"])

        def v(vector):
            mk.w(vector, sm, ref["sem"], ref["tok"])
            vector.tensor_copy(mk.maskb[:, :], mk.wload[0][:, 0:2048])
            vector.tensor_copy(mk.identb[:, :], mk.ldst[:, 0:128])
            vector.memset(mk.tmp1[:, 0:128], 1.0)
            vector.memset(mk.eps_rms[:, :], EPS_RMS)
            vector.memset(mk.eps_ln[:, :], EPS_LN)
            vector.memset(mk.zkw[:, :], 0.0)
            vector.drain()
            i = vector.tensor_copy(mk.ones_r[:, :], mk.tmp1[:, 0:128])
            mk.inc("sv", i)
        self.V(v)
        self.consts_tok = self.cnt["sv"]
        self.ref_consumed(ref, "sv", self.consts_tok)

    def cast_to_scratch(self, src, dst, ncols, wait_first):
        mk, sm = self, self.sems
        total = NDC * 128 * ncols
        tile = 128 * 4096
        nt = total // tile
        srcf = src[:, :].flatten()
        dstf = dst[:, :].flatten()
        store_tok = {-1: self.cnt["sgs"]}
        cast_tok = {}
        for dc in range(nt):
            par = dc % 2
            extra = [("sv", cast_tok[dc - 2])] if dc >= 2 else (
                [("sv", wait_first)] if wait_first else [])
            ref = self.sync_group(
                lambda sync, incf, dc=dc, par=par: incf(
                    sync.dma_start(mk.wload[par][:, 0:4096],
                                   srcf[dc * tile:(dc + 1) * tile]
                                   .rearrange("(p c) -> p c", p=128))),
                extra_waits=extra)

            def v(vector, dc=dc, par=par, ref=ref):
                mk.w(vector, sm, ref["sem"], ref["tok"])
                mk.w(vector, sm, "sgs", store_tok[dc - 1])
                i = vector.tensor_copy(mk.weff[:, 0:4096], mk.wload[par][:, 0:4096])
                mk.inc("sv", i)
            self.V(v)
            cast_tok[dc] = self.cnt["sv"]
            self.ref_consumed(ref, "sv", cast_tok[dc])

            def g(gp, dc=dc):
                mk.w(gp, sm, "sv", cast_tok[dc])
                i = gp.dma_start(dstf[dc * tile:(dc + 1) * tile]
                                 .rearrange("(p c) -> p c", p=128),
                                 mk.weff[:, 0:4096])
                mk.inc("sgs", i, 16)
            self.G(g)
            store_tok[dc] = self.cnt["sgs"]
        return store_tok[nt - 1]

    def emit_phase0(self):
        mk, sm = self, self.sems
        self.qkvT_store = self.cast_to_scratch(self.wqkvT, self.s_qkvT, 3 * D,
                                               self.consts_tok)
        self.woT_store = self.cast_to_scratch(self.woT_d, self.s_woT, D, self.cnt["sv"])

        svsnap0 = self.cnt["sv"]

        def dmas(sync, incf):
            incf(sync.dma_start(mk.wload[0][0:16, 0:1024], mk.v16_d[:, :]))
            incf(sync.dma_start(mk.wload[0][0:16, 1024:1040], mk.fbT_d[:, :]))
            incf(sync.dma_start(mk.wload[1][0:16, 0:4096], mk.uT_d[:, :]))
        ref = self.sync_group(dmas, extra_waits=[("sv", svsnap0)])

        def v(vector):
            mk.w(vector, sm, ref["sem"], ref["tok"])
            vector.drain()
            vector.tensor_copy(mk.v16b[:, :], mk.wload[0][0:16, 0:1024])
            vector.tensor_copy(mk.fbtb[:, :], mk.wload[0][0:16, 1024:1040])
            i = vector.tensor_copy(mk.utb[:, :], mk.wload[1][0:16, 0:4096])
            mk.inc("sv", i)
        self.V(v)
        lr_cast = self.cnt["sv"]
        self.ref_consumed(ref, "sv", lr_cast)

        def t(tensor):
            mk.w(tensor, sm, "sv", lr_cast)
            tensor.matmul(mk.psA[0:16, 0:512], mk.fbtb[:, :], mk.v16b[:, 0:512],
                          start=True, stop=True)
            i = tensor.matmul(mk.psA[0:16, 512:1024], mk.fbtb[:, :],
                              mk.v16b[:, 512:1024], start=True, stop=True)
            mk.inc("st", i)
        self.T(t)
        t1_mm = self.cnt["st"]

        def v2(vector):
            mk.w(vector, sm, "st", t1_mm)
            i = vector.tensor_copy(mk.t1b[:, :], mk.psA[0:16, 0:1024])
            mk.inc("sv", i)
        self.V(v2)
        self.t1_tok = self.cnt["sv"]

        self.ternarize(self.wfcT, self.wfcN, NDC, FF, 32, D, self.s_comb, with_lr=True)
        self.ternarize(self.wprojT, self.wprojN, FF // 128, D, 8, FF, self.s_proj, with_lr=False)

    def ternarize(self, wsrcT, wsrcN, nchunks, ncols, ntiles, ncolsN, sdst, with_lr):
        mk, sm = self, self.sems
        ngr = ncols // 512
        sv0 = self.cnt["sv"]
        ss0 = self.cnt["ss"]
        lvl0 = 32
        tps = 4096 // ncolsN
        nst = ntiles // tps
        tile_tok = {}
        for st_i in range(nst):
            par = st_i % 2
            extra = ([("sv", tile_tok[st_i - 2])] if st_i >= 2
                     else [("sv", sv0), ("ss", ss0)])
            ref = self.sync_group(
                lambda sync, incf, st_i=st_i, par=par: incf(
                    sync.dma_start(
                        mk.wload[par][:, 0:4096].rearrange("b (a c) -> b a c", a=tps),
                        wsrcN.rearrange("(a b) c -> b a c", b=128)
                        [:, st_i * tps:(st_i + 1) * tps, :])),
                extra_waits=extra)

            def sabs(scalar, st_i=st_i, par=par, ref=ref):
                mk.w(scalar, sm, ref["sem"], ref["tok"])
                mk.w(scalar, sm, "sv", mk.cnt["sv"])
                i = scalar.activation(out=mk.aw2f[:, 0:4096],
                                      in_=mk.wload[par][:, 0:4096],
                                      func=AF.Abs, bias=0.0, scale=1.0)
                mk.inc("ss", i)
            self.Sc(sabs)
            abs_tok = self.cnt["ss"]

            def vtree(vector, st_i=st_i, abs_tok=abs_tok):
                mk.w(vector, sm, "ss", abs_tok)
                a3 = mk.aw2f[:, 0:4096].rearrange("p (t c) -> p t c", c=ncolsN)
                n = ncolsN
                while n > lvl0:
                    h = n // 2
                    vector.tensor_tensor(a3[:, :, 0:h], a3[:, :, 0:h],
                                         a3[:, :, h:n], op=ADD)
                    vector.drain()
                    n = h
                i = vector.tensor_copy(
                    mk.surv[:, st_i * tps * lvl0:(st_i + 1) * tps * lvl0]
                    .rearrange("p (t c) -> p t c", c=lvl0),
                    a3[:, :, 0:lvl0])
                mk.inc("sv", i)
            self.V(vtree)
            tile_tok[st_i] = self.cnt["sv"]

        def vcomp2(vector):
            mk.w(vector, sm, "sv", tile_tok[nst - 1])
            T = ntiles
            sv_ = lambda c0, cn: mk.surv[:, 0:T * lvl0].rearrange(
                "p (t c) -> p t c", c=lvl0)[:, :, c0:c0 + cn]
            a3 = lambda c0, cn, h: mk.aw2f[:, c0:c0 + T * h].rearrange(
                "p (t c) -> p t c", c=h)
            eoff = 0
            err_cols = []
            n = lvl0
            while n > 1:
                h = n // 2
                a = sv_(0, h)
                b = sv_(h, n - h)
                stmp = mk.aw2f[:, 0:T * h].rearrange("p (t c) -> p t c", c=h)
                ztmp = mk.aw2f[:, 512:512 + T * h].rearrange("p (t c) -> p t c", c=h)
                wtmp = mk.aw2f[:, 1024:1024 + T * h].rearrange("p (t c) -> p t c", c=h)
                e1 = mk.aw2f[:, 1536:1536 + T * h].rearrange("p (t c) -> p t c", c=h)
                e2 = mk.aw2f[:, 2048:2048 + T * h].rearrange("p (t c) -> p t c", c=h)
                vector.tensor_tensor(stmp, a, b, op=ADD)
                vector.drain()
                vector.tensor_tensor(ztmp, stmp, a, op=SUB)
                vector.drain()
                vector.tensor_tensor(wtmp, stmp, ztmp, op=SUB)
                vector.drain()
                vector.tensor_tensor(e1, a, wtmp, op=SUB)
                vector.drain()
                vector.tensor_tensor(e2, b, ztmp, op=SUB)
                vector.drain()
                etot = mk.errs[:, eoff:eoff + T * h].rearrange("p (t c) -> p t c", c=h)
                vector.tensor_tensor(etot, e1, e2, op=ADD)
                vector.drain()
                err_cols.append((eoff, h))
                eoff += T * h
                vector.tensor_copy(sv_(0, h), stmp)
                vector.drain()
                n = h
            acc = mk.tmp1[:, 0:T]
            red = mk.tmp1[:, 64:64 + T]
            first = True
            for (c0, h) in err_cols:
                lv = mk.errs[:, c0:c0 + T * h].rearrange("p (t c) -> p t c", c=h)
                if h == 1:
                    src = mk.errs[:, c0:c0 + T]
                else:
                    vector.tensor_reduce(red, lv, axis=AX, op=ADD)
                    vector.drain()
                    src = red
                if first:
                    vector.tensor_copy(acc, src)
                else:
                    vector.tensor_tensor(acc, acc, src, op=ADD)
                vector.drain()
                first = False
            root = sv_(0, 1).squeeze(2)
            vector.tensor_tensor(mk.tmp1[:, 128:128 + T], root, acc, op=ADD)
            vector.drain()
            i = vector.tensor_scalar(mk.scaleN[:, 0:ntiles], mk.tmp1[:, 128:128 + T],
                                     1.0 / ncolsN, 1e-5, op0=MULT, op1=MAXOP)
            mk.inc("sv", i)
        self.V(vcomp2)
        scaleN_tok = self.cnt["sv"]

        sgs_snap = self.cnt["sgs"]

        def g1(gp):
            mk.w(gp, sm, "sv", scaleN_tok)
            gp.wait_ge(sm["sgs"], sgs_snap)
            i = gp.dma_start(mk.s_scale[0:ncols].rearrange("(t p) -> p t", p=128),
                             mk.scaleN[:, 0:ntiles], allow_slow_non_contiguous=True)
            tok = mk.inc("sgs", i, 16)
            gp.wait_ge(sm["sgs"], tok)
            i = gp.dma_start(mk.scf[0:1, 0:ncols], mk.s_scale[0:ncols].unsqueeze(0))
            tok = mk.inc("sgs", i, 16)
            gp.wait_ge(sm["sgs"], tok)
            rows = 1
            while rows < 128:
                nxt = min(2 * rows, 128)
                i = gp.dma_start(mk.scf[rows:nxt, 0:ncols],
                                 mk.scf[0:rows, 0:ncols][0:nxt - rows, :])
                tok = mk.inc("sgs", i, 16)
                gp.wait_ge(sm["sgs"], tok)
                rows = nxt
        self.G(g1)
        bcast_tok = self.cnt["sgs"]

        def vscb(vector):
            mk.w(vector, sm, "sgs", bcast_tok)
            i = vector.tensor_copy(mk.scb[:, 0:ncols], mk.scf[:, 0:ncols])
            mk.inc("sv", i)
        self.V(vscb)
        scale_tok = self.cnt["sv"]

        app_tok = {}
        store_tok = {-1: self.cnt["sgs"], -2: self.cnt["sgs"]}
        lr_tok = {}
        for c in range(nchunks):
            par = c % 2
            extra = [("sv", scale_tok if c < 2 else app_tok[c - 2]),
                     ("sgs", store_tok[c - 2])]
            ref = self.sync_group(
                lambda sync, incf, c=c, par=par: incf(
                    sync.dma_start(mk.wload[par][:, 0:ncols], wsrcT[ts(c, 128), :])),
                extra_waits=extra)
            ldn, lt = ref["sem"], ref["tok"]
            if with_lr:
                sgn = mk.weff[:, 0:ncols]
                res = mk.lrb[:, 0:ncols]
            else:
                sgn = mk.wload[par][:, 2048:4096].bitcast(BF16)[:, 0:ncols]
                res = sgn

            def s(scalar, c=c, par=par, ldn=ldn, lt=lt, sgn=sgn):
                mk.w(scalar, sm, ldn, lt)
                if c >= 1:
                    mk.w(scalar, sm, "sv", app_tok[c - 1])
                mk.w(scalar, sm, "sv", scale_tok)
                scalar.activation(out=mk.aw2f[:, 0:ncols],
                                  in_=mk.wload[par][:, 0:ncols],
                                  func=AF.Abs, bias=0.0, scale=2.0)
                i = scalar.activation(out=sgn,
                                      in_=mk.wload[par][:, 0:ncols],
                                      func=AF.Sign, bias=0.0, scale=1.0)
                mk.inc("ss", i)
            self.Sc(s)
            sgn_tok = self.cnt["ss"]

            if with_lr:
                def t(tensor, c=c):
                    mk.w(tensor, sm, "sv", mk.t1_tok)
                    if c == 0:
                        mk.w(tensor, sm, "sv", scale_tok)
                    if c >= 1:
                        mk.w(tensor, sm, "ss", lr_tok[c - 1])
                    for g in range(ngr):
                        i = tensor.matmul(mk.psB[:, ts(g % 4, 512)],
                                          mk.t1b[:, ts(c, 128)], mk.utb[:, ts(g, 512)],
                                          start=True, stop=True)
                    mk.inc("st", i)
                self.T(t)
                wlr_tok = self.cnt["st"]

                def s2(scalar, c=c, wlr_tok=wlr_tok):
                    mk.w(scalar, sm, "st", wlr_tok)
                    if c >= 1:
                        mk.w(scalar, sm, "sv", app_tok[c - 1])
                        mk.w(scalar, sm, "sgs", store_tok[c - 1])
                    for g in range(ngr):
                        i = scalar.activation(out=mk.lrb[:, ts(g, 512)],
                                              in_=mk.psB[:, ts(g % 4, 512)],
                                              func=AF.Copy, bias=0.0, scale=mk.gate)
                    mk.inc("ss", i)
                self.Sc(s2)
                lr_tok[c] = self.cnt["ss"]

            def v2(vector, c=c, par=par, ldn=ldn, lt=lt, sgn_tok=sgn_tok,
                   sgn=sgn, res=res):
                mk.w(vector, sm, ldn, lt)
                mk.w(vector, sm, "ss", sgn_tok if not with_lr else lr_tok[c])
                vector.drain()
                mb = [mk.rl[0][:, 0:512], mk.rl[1][:, 0:512],
                      mk.tmp1[:, 0:256].bitcast(BF16), mk.tmp1[:, 256:512].bitcast(BF16)]
                i = None
                for g0 in range(0, ngr, 4):
                    gs = range(g0, min(g0 + 4, ngr))
                    for g in gs:
                        vector.tensor_tensor(mb[g - g0], mk.aw2f[:, ts(g, 512)],
                                             mk.scf[:, ts(g, 512)], op=ISGT)
                    vector.drain()
                    for g in gs:
                        vector.tensor_tensor(mb[g - g0], mb[g - g0],
                                             sgn[:, ts(g, 512)], op=MULT)
                    vector.drain()
                    if with_lr:
                        for g in gs:
                            vector.tensor_tensor(mb[g - g0], mb[g - g0],
                                                 mk.scb[:, ts(g, 512)], op=MULT)
                        vector.drain()
                        for g in gs:
                            i = vector.tensor_tensor(res[:, ts(g, 512)], mb[g - g0],
                                                     mk.lrb[:, ts(g, 512)], op=ADD)
                    else:
                        for g in gs:
                            i = vector.tensor_tensor(res[:, ts(g, 512)], mb[g - g0],
                                                     mk.scb[:, ts(g, 512)], op=MULT)
                    if g0 + 4 < ngr:
                        vector.drain()
                mk.inc("sv", i)
            self.V(v2)
            app_tok[c] = self.cnt["sv"]
            self.ref_consumed(ref, "sv", app_tok[c])

            def g2(gp, c=c, res=res):
                mk.w(gp, sm, "sv", app_tok[c])
                mk.w(gp, sm, "sgs", store_tok[c - 1])
                i = gp.dma_start(sdst[ts(c, 128), :], res[:, 0:ncols])
                mk.inc("sgs", i, 16)
            self.G(g2)
            store_tok[c] = self.cnt["sgs"]
        if with_lr:
            self.comb_store = store_tok[nchunks - 1]
        else:
            self.proj_store = store_tok[nchunks - 1]

    def emit_seq(self, sq):
        mk, sm = self, self.sems
        if sq == 1:
            out_all = self.cnt["so"]

            def g(gp):
                gp.wait_ge(sm["so"], out_all)
                for dc in range(NDC):
                    i = gp.dma_start(mk.xb[:, ts(dc, S)], mk.xT[1, ts(dc, 128), :])
                    mk.inc("sgx", i, 16)
            self.G(g)
            self.xload_tok[1] = self.cnt["sgx"]

        self.emit_rms(sq)
        for p in range(NPAIR):
            self.emit_qkv_pair(sq, p)
            self.emit_attention_pair(sq, p)
        self.emit_wo_ln1(sq)
        pend = self.emit_ffn(sq)
        self._flush_ln2(pend)

    def emit_rms(self, sq):
        mk, sm = self, self.sems
        sq_tok = {}
        mm_tok = {}
        sv0 = self.cnt["sv"]
        for w in range(NW):
            for dc in range(NDC):
                slot = 0

                def s(scalar, w=w, dc=dc, slot=slot):
                    if w == 0 and dc == 0:
                        mk.w(scalar, sm, "sgx", mk.xload_tok[sq])
                        mk.w(scalar, sm, "sv", sv0)
                    if dc >= 1:
                        mk.w(scalar, sm, "st", mm_tok[(w, dc - 1)])
                    i = scalar.activation(out=mk.sq[slot][:, 0:512],
                                          in_=mk.xb[:, ts(dc, S)][:, ts(w, 512)],
                                          func=AF.Square, bias=0.0, scale=1.0)
                    mk.inc("ss", i)
                self.Sc(s)
                sq_tok[(w, dc)] = self.cnt["ss"]

                def t(tensor, w=w, dc=dc, slot=slot):
                    mk.w(tensor, sm, "ss", sq_tok[(w, dc)])
                    if w > 0 and dc == 0:
                        mk.w(tensor, sm, "ss", sqrt_tok[w - 1])
                    i = tensor.matmul(mk.psA[:, 0:512], mk.ones_r[:, :],
                                      mk.sq[slot][:, 0:512],
                                      start=(dc == 0), stop=(dc == NDC - 1))
                    mk.inc("st", i)
                self.T(t)
                mm_tok[(w, dc)] = self.cnt["st"]

            def s2(scalar, w=w):
                mk.w(scalar, sm, "st", mm_tok[(w, NDC - 1)])
                if w > 0:
                    mk.w(scalar, sm, "sv", mk.norm_tok[w - 1])
                i = scalar.activation(out=mk.stb[:, 0:512], in_=mk.psA[:, 0:512],
                                      func=AF.Sqrt, bias=mk.eps_rms[:, :], scale=1.0 / D)
                mk.inc("ss", i)
            self.Sc(s2)
            if w == 0:
                sqrt_tok = {}
            sqrt_tok[w] = self.cnt["ss"]

            def v(vector, w=w):
                mk.w(vector, sm, "ss", sqrt_tok[w])
                if sq == 0 and w == 0 and hasattr(mk, "proj_store"):
                    mk.w(vector, sm, "sgs", mk.proj_store)
                vector.reciprocal(mk.invr[:, 0:512], mk.stb[:, 0:512])
                vector.drain()
                for dc in range(NDC):
                    i = vector.tensor_tensor(mk.normed[:, ts(dc, S)][:, ts(w, 512)],
                                             mk.xb[:, ts(dc, S)][:, ts(w, 512)],
                                             mk.invr[:, 0:512], op=MULT)
                mk.inc("sv", i)
            self.V(v)
            if w == 0:
                self.norm_tok = {}
            self.norm_tok[w] = self.cnt["sv"]
        self.normed_done = self.cnt["sv"]

    def emit_qkv_pair(self, sq, p):
        mk, sm = self, self.sems
        slot = p % 2
        done_qk, done_v = set(), set()
        pre = getattr(self, "qkv_pre", None)
        if pre is not None and pre["pair"] == (sq, p):
            qref = pre["ref"]
            od = getattr(self, "ovl_done", None)
            if od is not None and od["pair"] == (sq, p):
                done_qk, done_v = od["qk"], od["v"]
        else:
            extra = []
            if sq == 0 and p < 2:
                extra.append(("sgs", mk.proj_store))
            if hasattr(mk, "qkv_mm_done") and (sq, p) != (0, 0):
                if p >= 2 and mk.qkv_mm_done.get((sq, p - 2)):
                    extra.append(("st", mk.qkv_mm_done[(sq, p - 2)]))
                if p < 2 and sq == 1 and mk.qkv_mm_done.get((0, p + NPAIR - 2)):
                    extra.append(("st", mk.qkv_mm_done[(0, p + NPAIR - 2)]))

            def dmas(sync, incf, p=p, slot=slot):
                for sec in range(3):
                    incf(sync.dma_start(
                        mk.wqkv[slot][:].rearrange("b (s a c) -> b s a c", s=3, a=NDC)
                        [:, sec:sec + 1, :, :],
                        mk.s_qkvT.rearrange("(a b) (s f) -> b s a f", b=128, s=3)
                        [:, sec:sec + 1, :, ts(p, 128)]))
            qref = self.sync_group(dmas, extra_waits=extra)
        ldn, wld = qref["sem"], qref["tok"]
        self.qref = qref
        self.ovl_done = None
        first = [True]

        def first_waits(tensor):
            mk.w(tensor, sm, ldn, wld)
            mk.w(tensor, sm, "sv", mk.normed_done)
            if (sq, p) != (0, 0):
                mk.w(tensor, sm, "ss", mk.cnt["ss"])
            first[0] = False

        qk_ev = {}
        for sec, dstbuf, bcol in ((0, self.qp, p), (1, self.kp, 8 + p)):
            for jw in range(NW):
                if (sec, jw) in done_qk:
                    continue
                bank = jw % 2
                fw = first[0]
                pred = qk_ev.get((sec, jw - 2) if jw >= 2 else (sec - 1, jw + 2))

                def t(tensor, sec=sec, jw=jw, bank=bank, fw=fw, pred=pred):
                    if fw:
                        first_waits(tensor)
                    if pred:
                        mk.w(tensor, sm, "sv", pred)
                    for dc in range(NDC):
                        i = tensor.matmul(mk.psA[:, ts(bank, 512)],
                                          mk.wqkv[p % 2][:, sec * D + dc * 128:
                                                         sec * D + dc * 128 + 128],
                                          mk.normed[:, ts(dc, S)][:, ts(jw, 512)],
                                          start=(dc == 0), stop=(dc == NDC - 1))
                    mk.inc("st", i)
                self.T(t)
                mmt = self.cnt["st"]

                def v(vector, jw=jw, bank=bank, mmt=mmt, dstbuf=dstbuf, bcol=bcol):
                    mk.w(vector, sm, "st", mmt)
                    i = vector.tensor_scalar(dstbuf[:, ts(jw, 512)],
                                             mk.psA[:, ts(bank, 512)],
                                             mk.bqkv_s[:, bcol:bcol + 1], None, op0=ADD)
                    mk.inc("sv", i)
                self.V(v)
                qk_ev[(sec, jw)] = self.cnt["sv"]
        qk_mid = self.cnt["sv"]

        v_ev = {}
        for tt in range(NKT):
            if tt in done_v:
                continue
            bank = tt % 2
            fw = first[0]
            pred = v_ev.get(tt - 2)

            def t(tensor, tt=tt, bank=bank, fw=fw, pred=pred):
                if fw:
                    first_waits(tensor)
                if pred:
                    mk.w(tensor, sm, "sv", pred)
                for dc in range(NDC):
                    i = tensor.matmul(mk.psA[:, 1024 + bank * 512: 1024 + bank * 512 + 128],
                                      mk.normed[:, ts(dc, S)][:, ts(tt, 128)],
                                      mk.wqkv[p % 2][:, 2 * D + dc * 128: 2 * D + dc * 128 + 128],
                                      start=(dc == 0), stop=(dc == NDC - 1))
                mk.inc("st", i)
            self.T(t)
            mmt = self.cnt["st"]

            def v(vector, tt=tt, bank=bank, mmt=mmt):
                mk.w(vector, sm, "st", mmt)
                i = vector.tensor_tensor(mk.vp[:, ts(tt, 128)],
                                         mk.psA[:, 1024 + bank * 512: 1024 + bank * 512 + 128],
                                         mk.bvb[:, ts(p, 128)], op=ADD)
                mk.inc("sv", i)
            self.V(v)
            v_ev[tt] = self.cnt["sv"]
        if not hasattr(self, "qkv_mm_done"):
            self.qkv_mm_done = {}
        self.qkv_mm_done[(sq, p)] = self.cnt["st"]
        self.ref_consumed(self.qref, "st", self.cnt["st"])
        self.qkv_ev_done = self.cnt["sv"]
        self.qk_ev_done = qk_mid

    def emit_attention_pair(self, sq, p):
        mk, sm = self, self.sems
        first_pair = not hasattr(self, "gkt")
        if first_pair:
            self.gkt = 0
            self.av_tok = {}
            self.zevac_tok = {}
        qkv_v = self.qkv_ev_done
        qkv_s = self.qk_ev_done
        self._kw_bank = None
        self._kw_pending = None

        if first_pair or p == 0:
            def vz(vector):
                if not first_pair:
                    mk.w(vector, sm, "sv", self.zevac_tok[(sq - 1, NPAIR - 1)])
                    vector.drain()
                i = vector.memset(mk.psB[64:128, :], 0.0)
                mk.inc("sv", i)
            self.V(vz)
            zinit_tok = self.cnt["sv"]
        else:
            zinit_tok = None

        ovl_q = []
        ovl_banks = {}
        ovl_ready = {}
        ovl_first = [True]
        ovl_started = [False]
        ovl_done_qk, ovl_done_v = set(), set()
        if p + 1 < NPAIR:
            np_ = p + 1
            nslot = np_ % 2
            guard = []
            if p >= 1:
                guard.append(("st", self.qkv_mm_done[(sq, p - 1)]))
            elif sq >= 1:
                guard.append(("st", self.qkv_mm_done[(sq - 1, NPAIR - 1)]))
            if sq == 0 and np_ < 2:
                guard.append(("sgs", mk.proj_store))

            def pdmas(sync, incf, np_=np_, nslot=nslot):
                for sec in range(3):
                    incf(sync.dma_start(
                        mk.wqkv[nslot][:].rearrange("b (s a c) -> b s a c", s=3, a=NDC)
                        [:, sec:sec + 1, :, :],
                        mk.s_qkvT.rearrange("(a b) (s f) -> b s a f", b=128, s=3)
                        [:, sec:sec + 1, :, ts(np_, 128)]))
            pref = self.sync_group(pdmas, extra_waits=guard)
            self.qkv_pre = {"pair": (sq, np_), "ref": pref}
            ovl_q = [("v", t) for t in range(16)]

        UL = []
        for kt in range(NKT):
            j0 = kt >> 2
            nq = S - 512 * j0
            ups = [(512 * j0, min(1024, nq))]
            if nq > 1024:
                ups.append((512 * j0 + 1024, nq - 1024))
            for u, (qs, wdt) in enumerate(ups):
                UL.append((kt, u, qs, wdt, u == 0, u == len(ups) - 1))
        NU = len(UL)
        gkt0 = self.gkt
        sets = [mk.psA[:, 0:1024], mk.psA[:, 1024:2048]]
        exp_tok = {}
        sc_tok = {}
        vt_tok = {}
        pend_q = []

        def flush_av(need_gkt=None, max_age=2):
            while pend_q and (pend_q[0][1] >= max_age or
                              (need_gkt is not None and pend_q[0][0] <= need_gkt)):
                pend_q.pop(0)[2]()

        def age_av():
            for k in range(len(pend_q)):
                pend_q[k] = (pend_q[k][0], pend_q[k][1] + 1, pend_q[k][2])

        def emit_scores_pair(tensor, units_):
            chunks = []
            for hh, ui in units_:
                kt, u, qs, wdt, first, last = UL[ui]
                if first:
                    tensor.matmul(sets[hh][:, 0:512], mk.identb[:, :],
                                  mk.maskb[:, ts(kt % 4, 512)],
                                  start=True, stop=False)
                for n in range(wdt // 512):
                    chunks.append((hh, ui, n))
            i2 = None
            for ci in range(max(len([c for c in chunks if c[0] == 0]),
                                len([c for c in chunks if c[0] == 1]))):
                for hh in range(2):
                    hc = [c for c in chunks if c[0] == hh]
                    if ci < len(hc):
                        _, ui, n = hc[ci]
                        kt, u, qs, wdt, first, last = UL[ui]
                        po = 64 * hh
                        i2 = tensor.matmul(
                            sets[hh][:, ts(n, 512)],
                            mk.kp[po:po + 64, ts(kt, 128)],
                            mk.qp[po:po + 64, qs + 512 * n: qs + 512 * n + 512],
                            start=(not (first and n == 0)), stop=True)
            return i2

        def emit_ovl_group(g, bank, wtok):
            np_ = p + 1
            nslot = np_ % 2
            pref = self.qkv_pre["ref"]
            fw = ovl_first[0]
            ovl_first[0] = False
            ovl_started[0] = True
            if g[0] == "qk":
                _, sec, w = g

                def t(tensor, sec=sec, w=w, bank=bank, wtok=wtok, fw=fw):
                    if fw:
                        mk.w(tensor, sm, pref["sem"], pref["tok"])
                    if wtok:
                        mk.w(tensor, sm, "sv", wtok)
                    for dc in range(NDC):
                        i2 = tensor.matmul(mk.psB[:, ts(bank, 512)],
                                           mk.wqkv[nslot][:, sec * D + dc * 128:
                                                          sec * D + dc * 128 + 128],
                                           mk.normed[:, ts(dc, S)][:, ts(w, 512)],
                                           start=(dc == 0), stop=(dc == NDC - 1))
                    mk.inc("st", i2)
                self.T(t)
                mmt = self.cnt["st"]

                def v(vector, sec=sec, w=w, bank=bank, mmt=mmt):
                    mk.w(vector, sm, "st", mmt)
                    dst = mk.qp if sec == 0 else mk.kp
                    bcol = np_ if sec == 0 else 8 + np_
                    i2 = vector.tensor_scalar(dst[:, ts(w, 512)],
                                              mk.psB[:, ts(bank, 512)],
                                              mk.bqkv_s[:, bcol:bcol + 1], None,
                                              op0=ADD)
                    mk.inc("sv", i2)
                self.V(v)
                ovl_done_qk.add((sec, w))
            else:
                _, tt = g

                def t(tensor, tt=tt, bank=bank, wtok=wtok, fw=fw):
                    if fw:
                        mk.w(tensor, sm, pref["sem"], pref["tok"])
                    if wtok:
                        mk.w(tensor, sm, "sv", wtok)
                    for dc in range(NDC):
                        i2 = tensor.matmul(mk.psB[:, bank * 512: bank * 512 + 128],
                                           mk.normed[:, ts(dc, S)][:, ts(tt, 128)],
                                           mk.wqkv[nslot][:, 2 * D + dc * 128:
                                                          2 * D + dc * 128 + 128],
                                           start=(dc == 0), stop=(dc == NDC - 1))
                    mk.inc("st", i2)
                self.T(t)
                mmt = self.cnt["st"]

                def v(vector, tt=tt, bank=bank, mmt=mmt):
                    mk.w(vector, sm, "st", mmt)
                    i2 = vector.tensor_tensor(mk.vp[:, ts(tt, 128)],
                                               mk.psB[:, bank * 512: bank * 512 + 128],
                                               mk.bvb[:, ts(np_, 128)], op=ADD)
                    mk.inc("sv", i2)
                self.V(v)
                ovl_done_v.add(tt)
            ovl_banks[bank] = ("idle", self.cnt["sv"])

        def try_ovl_group():
            if not ovl_q:
                return False
            bank = wtok = None
            for b_ in (0, 1, 2):
                st_ = ovl_banks.get(b_)
                if st_ is not None and st_[0] == "idle":
                    bank, wtok = b_, st_[1]
                    break
            if bank is None:
                return False
            for qi_ in range(len(ovl_q)):
                g = ovl_q[qi_]
                if g[0] == "qk":
                    if g[2] not in ovl_ready:
                        continue
                elif g[1] not in vt_tok:
                    continue
                ovl_q.pop(qi_)
                emit_ovl_group(g, bank, wtok)
                return True
            return False

        for i in range(NU + 1):
            age_av()
            flush_av()

            if not try_ovl_group() and not ovl_started[0] and KW_ENABLE:
                if self._kw_pending is not None:
                    kb, kev = self._kw_pending
                    self._kw_pending = None

                    def tkw0(tensor, kb=kb, kev=kev):
                        mk.w(tensor, sm, "sv", kev)
                        i2 = tensor.matmul(mk.psB[:, ts(kb, 512)], mk.zkw[:, :],
                                           mk.maskb[:, 0:512], start=True, stop=True)
                        mk.inc("st", i2)
                    self.T(tkw0)
                    self._kw_bank = kb
                elif self._kw_bank is not None:
                    def tkw(tensor, kb=self._kw_bank):
                        for _ in range(3):
                            tensor.matmul(mk.psB[:, ts(kb, 512)], mk.zkw[:, :],
                                          mk.maskb[:, 0:512], start=True, stop=True)
                    self.T(tkw)

            if i >= 1:
                def t1(tensor, i=i):
                    if i == 1:
                        mk.w(tensor, sm, "sv", qkv_v)
                    if exp_tok.get((1, i - 2)):
                        mk.w(tensor, sm, "ss", exp_tok[(1, i - 2)])
                    mk.inc("st", emit_scores_pair(tensor, [(1, i - 1)]))
                self.T(t1)
                sc_tok[(1, i - 1)] = self.cnt["st"]
            if i < NU:
                def t0(tensor, i=i):
                    if exp_tok.get((0, i - 1)):
                        mk.w(tensor, sm, "ss", exp_tok[(0, i - 1)])
                    if i == 0:
                        mk.w(tensor, sm, "sv", qkv_s)
                        if zinit_tok:
                            mk.w(tensor, sm, "sv", zinit_tok)
                    mk.inc("st", emit_scores_pair(tensor, [(0, i)]))
                self.T(t0)
                sc_tok[(0, i)] = self.cnt["st"]

            for hh, ui in ((1, i - 1), (0, i)):
                if 0 <= ui < NU and UL[ui][4]:
                    def v(vector, hh=hh, ui=ui):
                        kt = UL[ui][0]
                        mk.w(vector, sm, "st", sc_tok[(hh, ui)])
                        dl = 128 * (kt % 4)
                        i2 = vector.reduce_max(mk.negc[:, hh:hh + 1],
                                               sets[hh][:, dl:dl + 128],
                                               axis=AX, negate=True)
                        mk.inc("sv", i2)
                    self.V(v)
                    if not hasattr(self, "_negc_tok"):
                        self._negc_tok = {}
                    self._negc_tok[(hh, UL[ui][0])] = self.cnt["sv"]

            for hh, ui in ((1, i - 1), (0, i)):
                if not (0 <= ui < NU):
                    continue
                kt, u, qs, wdt, first, last = UL[ui]
                gkt = gkt0 + kt

                if first:
                    flush_av(need_gkt=gkt - 2)

                def sx(scalar, hh=hh, ui=ui, kt=kt, u=u, qs=qs, wdt=wdt,
                       first=first, gkt=gkt):
                    mk.w(scalar, sm, "st", sc_tok[(hh, ui)])
                    mk.w(scalar, sm, "sv", self._negc_tok[(hh, kt)])
                    if first and self.av_tok.get(gkt - 2):
                        mk.w(scalar, sm, "st", self.av_tok[gkt - 2])
                    abf = mk.abuf[2 * (gkt % 2) + hh]
                    j0 = kt >> 2
                    i2 = scalar.activation(
                        out=abf[:, qs - 512 * j0: qs - 512 * j0 + wdt],
                        in_=sets[hh][:, 0:wdt], func=AF.Exp,
                        bias=mk.negc[:, hh:hh + 1], scale=1.0,
                        accum_out=mk.ztm[:, hh * 4 + u: hh * 4 + u + 1])
                    mk.inc("ss", i2)
                self.Sc(sx)
                exp_tok[(hh, ui)] = self.cnt["ss"]

                if last:
                    flush_av(need_gkt=gkt - 2)

                    def v2(vector, hh=hh, kt=kt, u=u, gkt=gkt):
                        mk.w(vector, sm, "ss", exp_tok[(hh, ui)] if False
                             else self.cnt["ss"])
                        if self.av_tok.get(gkt - 2):
                            mk.w(vector, sm, "st", self.av_tok[gkt - 2])
                        if u > 0:
                            vector.tensor_tensor(
                                mk.zts[:, hh:hh + 1],
                                mk.ztm[:, hh * 4: hh * 4 + 1],
                                mk.ztm[:, hh * 4 + 1: hh * 4 + 2], op=ADD)
                            vector.drain()
                            vector.reciprocal(mk.rz[:, hh:hh + 1],
                                              mk.zts[:, hh:hh + 1])
                        else:
                            vector.reciprocal(mk.rz[:, hh:hh + 1],
                                              mk.ztm[:, hh * 4: hh * 4 + 1])
                        vector.drain()
                        po = 64 * hh
                        i2 = vector.tensor_scalar(
                            mk.vt[:, ts(2 * (gkt % 2) + hh, 64)],
                            mk.vp[:, kt * 128 + po: kt * 128 + po + 64],
                            mk.rz[:, hh:hh + 1], None, op0=MULT)
                        mk.inc("sv", i2)
                    self.V(v2)
                    if hh == 1:
                        vt_tok[kt] = self.cnt["sv"]

                        def t2(tensor, kt=kt, gkt=gkt):
                            mk.w(tensor, sm, "sv", vt_tok[kt])
                            j0 = kt >> 2
                            i2 = None
                            for j in range(j0, 4):
                                for h2 in range(2):
                                    i2 = tensor.matmul(
                                        mk.psB[64 * h2:64 * h2 + 64, ts(j, 512)],
                                        mk.vt[:, ts(2 * (gkt % 2) + h2, 64)],
                                        mk.abuf[2 * (gkt % 2) + h2][:, ts(j - j0, 512)],
                                        start=(kt == 0 and h2 == 0),
                                        stop=(kt == 4 * j + 3 and h2 == 1),
                                        tile_position=(0, 64 * h2),
                                        skip_group_check=True)
                            mk.inc("st", i2)

                        def mkpend(t2=t2, gkt=gkt, kt=kt):
                            def run():
                                self.T(t2)
                                self.av_tok[gkt] = self.cnt["st"]
                                if kt % 4 == 3 and EARLY_EVAC:
                                    b = kt >> 2

                                    def ve(vector, b=b, gkt=gkt):
                                        mk.w(vector, sm, "st", self.av_tok[gkt])
                                        vector.tensor_copy(
                                            mk.zbuf[:, p * S + 512 * b:
                                                    p * S + 512 * b + 512],
                                            mk.psB[:, ts(b, 512)])
                                        vector.drain()
                                        i3 = vector.memset(
                                            mk.psB[64:128, ts(b, 512)], 0.0)
                                        mk.inc("sv", i3)
                                    self.V(ve)
                                    if b == 3:
                                        self.zevac_tok[(sq, p)] = self.cnt["sv"]
                                    else:
                                        self._kw_pending = (b, self.cnt["sv"])
                                        ovl_ready[b] = self.cnt["sv"]
                                        ovl_banks[b] = ("idle", self.cnt["sv"])
                            return run
                        pend_q.append((gkt, 0, mkpend()))

        flush_av(max_age=0)
        self.gkt = gkt0 + NKT
        for b_ in sorted(ovl_banks):
            tok_ = ovl_banks[b_][1]

            def tz(tensor, b_=b_, tok_=tok_):
                mk.w(tensor, sm, "sv", tok_)
                i2 = tensor.matmul(mk.psB[:, ts(b_, 512)], mk.zkw[:, :],
                                   mk.maskb[:, 0:512], start=True, stop=True)
                mk.inc("st", i2)
            self.T(tz)
        if p + 1 < NPAIR:
            self.ovl_done = {"pair": (sq, p + 1), "qk": ovl_done_qk,
                             "v": ovl_done_v}
        if not EARLY_EVAC:
            def v3(vector, p=p):
                mk.w(vector, sm, "st", self.av_tok[self.gkt - 1])
                for j in range(4):
                    vector.tensor_copy(mk.zbuf[:, p * S + 512 * j: p * S + 512 * j + 512],
                                       mk.psB[:, ts(j, 512)])
                vector.drain()
                i = vector.memset(mk.psB[64:128, :], 0.0)
                mk.inc("sv", i)
            self.V(v3)
            self.zevac_tok[(sq, p)] = self.cnt["sv"]

    def emit_wo_ln1(self, sq):
        mk, sm = self, self.sems
        zdone = self.zevac_tok[(sq, NPAIR - 1)]
        att_done = self.av_tok[self.gkt - 1]

        def dmas(sync, incf):
            for dc in range(NDC):
                incf(sync.dma_start(mk.wot[:, ts(dc, D)], mk.s_woT[ts(dc, 128), :]))
        wref = self.sync_group(dmas, extra_waits=[("st", att_done), ("sv", zdone)])

        ev_tok = {}
        pend_apply = None
        for jw in range(NW):
            for oc in range(NDC):
                bank = 2 + (oc % 2)

                def t(tensor, jw=jw, oc=oc, bank=bank):
                    if jw == 0 and oc == 0:
                        mk.w(tensor, sm, wref["sem"], wref["tok"])
                        mk.w(tensor, sm, "sv", zdone)
                    if (jw, oc) > (0, 1) or (jw == 0 and oc > 1):
                        prev = (jw, oc - 2) if oc >= 2 else (jw - 1, NDC - 2 + oc)
                        mk.w(tensor, sm, "sv", ev_tok[prev])
                    for zc in range(NDC):
                        i = tensor.matmul(mk.psA[:, ts(bank, 512)],
                                          mk.wot[:, zc * D + oc * 128: zc * D + oc * 128 + 128],
                                          mk.zbuf[:, ts(zc, S)][:, ts(jw, 512)],
                                          start=(zc == 0), stop=(zc == NDC - 1))
                    mk.inc("st", i)
                self.T(t)
                mmt = self.cnt["st"]

                def v(vector, jw=jw, oc=oc, bank=bank, mmt=mmt):
                    mk.w(vector, sm, "st", mmt)
                    i = vector.scalar_tensor_tensor(
                        out=mk.xfr(oc, jw),
                        in0=mk.psA[:, ts(bank, 512)],
                        scalar=mk.bo_s[:, oc:oc + 1],
                        in1=mk.xf(oc, jw),
                        op0=ADD, op1=ADD)
                    mk.inc("sv", i)
                self.V(v)
                ev_tok[(jw, oc)] = self.cnt["sv"]
            if pend_apply is not None:
                pend_apply()
            pend_apply = self.emit_ln(sq, jw, last_writer=ev_tok[(jw, NDC - 1)],
                                      eps=EPS_LN, to_out=False, phase="ln1", jw=jw)
        pend_apply()
        self.ref_consumed(wref, "st", self.cnt["st"])

    def emit_ln(self, sq, w, last_writer, eps, to_out, phase, jw):
        mk, sm = self, self.sems
        sq_tok = {}
        mm_tok = {}
        for dc in range(NDC):
            slot = 0

            def s(scalar, dc=dc, slot=slot):
                mk.w(scalar, sm, "sv", last_writer)
                if dc >= 1:
                    mk.w(scalar, sm, "st", mm_tok[dc - 1])
                i = scalar.activation(out=mk.sq[slot][:, 0:512],
                                      in_=mk.yf(dc, jw),
                                      func=AF.Square, bias=0.0, scale=1.0)
                mk.inc("ss", i)
            self.Sc(s)
            sq_tok[dc] = self.cnt["ss"]

            def t(tensor, dc=dc, slot=slot):
                mk.w(tensor, sm, "ss", sq_tok[dc])
                if dc == 0:
                    mk.w(tensor, sm, "sv", last_writer)
                tensor.matmul(mk.psA[:, 0:512], mk.ones_r[:, :],
                              mk.xfr(dc, jw),
                              start=(dc == 0), stop=(dc == NDC - 1))
                i = tensor.matmul(mk.psA[:, 512:1024], mk.ones_r[:, :],
                                  mk.sq[slot][:, 0:512],
                                  start=(dc == 0), stop=(dc == NDC - 1))
                mk.inc("st", i)
            self.T(t)
            mm_tok[dc] = self.cnt["st"]

        slot = jw % 2

        def v(vector, slot=slot):
            mk.w(vector, sm, "st", mm_tok[NDC - 1])
            if hasattr(self, "ln_apply_tok") and self.ln_apply_tok.get(slot):
                mk.w(vector, sm, "sv", self.ln_apply_tok[slot])
            vector.tensor_scalar(mk.lnm[slot][:, 0:512], mk.psA[:, 0:512], 1.0 / D,
                                 None, op0=MULT)
            vector.tensor_scalar(mk.lnr[slot][:, 0:512], mk.psA[:, 512:1024], 1.0 / D,
                                 None, op0=MULT)
            vector.drain()
            vector.tensor_tensor(mk.tmp1[:, 0:512], mk.lnm[slot][:, 0:512],
                                 mk.lnm[slot][:, 0:512], op=MULT)
            vector.drain()
            i = vector.tensor_tensor(mk.lnr[slot][:, 0:512], mk.lnr[slot][:, 0:512],
                                     mk.tmp1[:, 0:512], op=SUB)
            mk.inc("sv", i)
        self.V(v)
        var_tok = self.cnt["sv"]

        def s2(scalar, slot=slot):
            mk.w(scalar, sm, "sv", var_tok)
            i = scalar.activation(out=mk.tmp1[:, 0:512], in_=mk.lnr[slot][:, 0:512],
                                  func=AF.Sqrt,
                                  bias=(mk.eps_ln if eps == EPS_LN else mk.eps_rms)[:, :],
                                  scale=1.0)
            mk.inc("ss", i)
        self.Sc(s2)
        sd_tok = self.cnt["ss"]

        def v2(vector, slot=slot):
            mk.w(vector, sm, "ss", sd_tok)
            i = vector.reciprocal(mk.lnr[slot][:, 0:512], mk.tmp1[:, 0:512])
            mk.inc("sv", i)
        self.V(v2)
        rstd_tok = self.cnt["sv"]

        def apply_emit():
            ap_tok = {}
            for dc in range(NDC):
                def v3(vector, dc=dc, slot=slot):
                    if dc == 0:
                        mk.w(vector, sm, "sv", rstd_tok)
                    vector.tensor_tensor(mk.tmp1[:, 0:512],
                                         mk.yf(dc, jw),
                                         mk.lnm[slot][:, 0:512], op=SUB)
                    vector.drain()
                    out_ap = mk.xfr(dc, jw)
                    i = vector.tensor_tensor(out_ap, mk.tmp1[:, 0:512],
                                             mk.lnr[slot][:, 0:512], op=MULT)
                    vector.drain()
                    mk.inc("sv", i)
                self.V(v3)
                ap_tok[dc] = self.cnt["sv"]
            if not hasattr(self, "ln_apply_tok"):
                self.ln_apply_tok = {}
            self.ln_apply_tok[slot] = self.cnt["sv"]
            if phase == "ln1":
                if not hasattr(self, "ln1_tok"):
                    self.ln1_tok = {}
                self.ln1_tok[(sq, jw)] = self.cnt["sv"]
            else:
                for dc in range(NDC):
                    def g(gp, dc=dc):
                        mk.w(gp, sm, "sv", ap_tok[dc])
                        i = gp.dma_start(mk.outT[sq, ts(dc, 128), ts(jw, 512)],
                                         mk.yf(dc, jw))
                        mk.inc("so", i, 16)
                    self.G(g)
        return apply_emit

    def emit_ffn(self, sq):
        mk, sm = self, self.sems
        ln1_all = self.ln1_tok[(sq, NW - 1)]
        qkv_last = self.qkv_mm_done[(sq, NPAIR - 1)]

        extra1 = [("st", qkv_last)]
        if sq == 0:
            extra1.append(("sgs", self.comb_store))
            extra1.append(("sgs", self.proj_store))

        def dmas1(sync, incf):
            for dc in range(4):
                incf(sync.dma_start(mk.wcomb[:, ts(dc, FF)], mk.s_comb[ts(dc, 128), :]))
        cref1 = self.sync_group(dmas1, extra_waits=extra1)

        def dmas2(sync, incf):
            for dc in range(4, NDC):
                incf(sync.dma_start(mk.wcomb[:, ts(dc, FF)], mk.s_comb[ts(dc, 128), :]))
        cref2 = self.sync_group(
            dmas2, extra_waits=[("sv", self.zevac_tok[(sq, NPAIR - 1)]),
                                ("st", self.cnt["st"])])

        relu_tok = {}
        r2_tok = {}
        pend_ln2 = None
        self.x2b_tok = getattr(self, "x2b_tok", {})
        for w in range(NW):
            f1_first = {}

            def sc_cast(scalar, w=w):
                mk.w(scalar, sm, "sv", ln1_all)
                if f1_first.get("prev") is not None:
                    mk.w(scalar, sm, "st", f1_first["prev"])
                i = None
                for dc in range(NDC):
                    i = scalar.activation(out=mk.x2bw[:, ts(dc, 512)],
                                          in_=mk.yf(dc, w), func=AF.Copy,
                                          bias=0.0, scale=1.0)
                mk.inc("ss", i)
            if (w > 0 or sq > 0) and hasattr(self, "f1_done") and self.f1_done:
                f1_first["prev"] = self.f1_done
            self.Sc(sc_cast)
            self.x2b_tok[(sq, w)] = self.cnt["ss"]

            for hc in range(32):
                bank = hc % 4

                svsnap = self.cnt["sv"]

                def t(tensor, w=w, hc=hc, bank=bank, svsnap=svsnap):
                    if w == 0 and hc == 0:
                        mk.w(tensor, sm, cref1["sem"], cref1["tok"])
                        mk.w(tensor, sm, cref2["sem"], cref2["tok"])
                        mk.w(tensor, sm, "sv", ln1_all)
                    if w > 0 and hc == 0:
                        mk.w(tensor, sm, "sv", svsnap)
                    if hc == 0:
                        mk.w(tensor, sm, "ss", self.x2b_tok[(sq, w)])
                    if (w, hc) >= (0, 4):
                        prev = (w, hc - 4) if hc >= 4 else (w - 1, 28 + hc)
                        if prev in relu_tok:
                            mk.w(tensor, sm, "ss", relu_tok[prev])
                    for dc in range(NDC):
                        i = tensor.matmul(mk.psA[:, ts(bank, 512)],
                                          mk.wcomb[:, dc * FF + hc * 128: dc * FF + hc * 128 + 128],
                                          mk.x2bw[:, ts(dc, 512)],
                                          start=(dc == 0), stop=(dc == NDC - 1))
                    mk.inc("st", i)
                self.T(t)
                mmt = self.cnt["st"]

                def s(scalar, w=w, hc=hc, bank=bank, mmt=mmt):
                    mk.w(scalar, sm, "st", mmt)
                    if hc >= 2 or w > 0:
                        prev = (w, hc - 2) if hc >= 2 else (w - 1, 30 + hc)
                        if prev in r2_tok:
                            mk.w(scalar, sm, "sv", r2_tok[prev])
                    i = scalar.activation(out=mk.rl[hc % 2][:, 0:512],
                                          in_=mk.psA[:, ts(bank, 512)],
                                          func=AF.Relu, bias=0.0, scale=1.0)
                    mk.inc("ss", i)
                self.Sc(s)
                relu_tok[(w, hc)] = self.cnt["ss"]

                def v(vector, w=w, hc=hc, mmt=mmt):
                    mk.w(vector, sm, "ss", relu_tok[(w, hc)])
                    i = vector.tensor_tensor(mk.r2[:, ts(hc, 512)],
                                             mk.rl[hc % 2][:, 0:512],
                                             mk.rl[hc % 2][:, 0:512], op=MULT)
                    mk.inc("sv", i)
                self.V(v)
                r2_tok[(w, hc)] = self.cnt["sv"]

            self.f1_done = self.cnt["st"]

            pj_mm = {}
            for hcp in range(16):
                hc0 = hcp * 2
                pslot = hcp % 2
                if hcp >= 2:
                    extra2 = [("st", pj_mm[hc0 - 3])]
                elif w > 0:
                    extra2 = [("st", self.pj_done[w - 1])]
                else:
                    extra2 = [("st", self.cnt["st"])]
                pref = self.sync_group(
                    lambda sync, incf, hc0=hc0, pslot=pslot: incf(
                        sync.dma_start(
                            mk.wpr0[:, pslot * 2 * D:(pslot + 1) * 2 * D]
                            .rearrange("b (a c) -> b a c", a=2),
                            mk.s_proj.rearrange("(a b) c -> b a c", b=128)
                            [:, hc0:hc0 + 2, :])),
                    extra_waits=extra2)
                ldn, wt = pref["sem"], pref["tok"]

                for hc in (hc0, hc0 + 1):
                    slot = 2 * pslot + (hc % 2)

                    def t2(tensor, w=w, hc=hc, slot=slot, ldn=ldn, wt=wt):
                        if hc % 2 == 0:
                            mk.w(tensor, sm, ldn, wt)
                        mk.w(tensor, sm, "sv", r2_tok[(w, hc)])
                        if hc == 0:
                            mk.w(tensor, sm, "ss", relu_tok[(w, 31)])
                            if w > 0:
                                mk.w(tensor, sm, "sv", self.y2_tok[(w - 1, NDC - 1)])
                        for oc in range(NDC):
                            ps = mk.psA if oc < 4 else mk.psB
                            i = tensor.matmul(ps[:, ts(oc % 4, 512)],
                                              mk.wpr0[:, slot * D + oc * 128:
                                                      slot * D + oc * 128 + 128],
                                              mk.r2[:, ts(hc, 512)],
                                              start=(hc == 0), stop=(hc == 31))
                        mk.inc("st", i)
                    self.T(t2)
                    pj_mm[hc] = self.cnt["st"]
                self.ref_consumed(pref, "st", pj_mm[hc0 + 1])
            self.pj_done = getattr(self, "pj_done", {})
            self.pj_done[w] = self.cnt["st"]

            self.y2_tok = getattr(self, "y2_tok", {})
            for oc in range(NDC):
                def v2(vector, w=w, oc=oc):
                    mk.w(vector, sm, "st", self.pj_done[w])
                    ps = mk.psA if oc < 4 else mk.psB
                    i = vector.tensor_tensor(
                        mk.xfr(oc, w),
                        ps[:, ts(oc % 4, 512)],
                        mk.yf(oc, w), op=ADD)
                    mk.inc("sv", i)
                self.V(v2)
                self.y2_tok[(w, oc)] = self.cnt["sv"]

            if pend_ln2 is not None:
                pend_ln2()
            pend_ln2 = self.emit_ln(sq, w, last_writer=self.y2_tok[(w, NDC - 1)],
                                    eps=EPS_LN, to_out=True, phase="ln2", jw=w)
            if w == 0:
                self.ref_consumed(cref1, "st", self.cnt["st"])
                self.ref_consumed(cref2, "st", self.cnt["st"])
        return pend_ln2


    def _flush_ln2(self, pend):
        if pend is not None:
            pend()

    def emit_final(self):
        mk, sm = self, self.sems
        total = self.cnt["so"]

        def g(gp):
            gp.wait_ge(sm["so"], total)
        self.G(g)


def _prep_inputs(inputs):
    x = np.asarray(inputs["L0_attn_x"], dtype=np.float32)
    W_qkv = np.asarray(inputs["L0_attn_W_qkv"], dtype=np.float32)
    b_qkv = np.asarray(inputs["L0_attn_b_qkv"], dtype=np.float32)
    W_o = np.asarray(inputs["L0_attn_W_o"], dtype=np.float32)
    b_o = np.asarray(inputs["L0_attn_b_o"], dtype=np.float32)
    fb = np.asarray(inputs["L0_ffn_fast_b"], dtype=np.float32)
    w_fc = np.asarray(inputs["L0_ffn_w_fc"], dtype=np.float32)
    w_proj = np.asarray(inputs["L0_ffn_w_proj"], dtype=np.float32)
    u = np.asarray(inputs["L0_ffn_u"], dtype=np.float32)
    v = np.asarray(inputs["L0_ffn_v"], dtype=np.float32)
    gate = float(np.asarray(inputs["L0_ffn_gate"]))

    hh = np.arange(H).repeat(DH)
    dd = np.tile(np.arange(DH), H)
    idx = np.concatenate([hh * 192 + dd * 3 + s for s in range(3)])
    Wp = W_qkv[idx]
    bp = b_qkv[idx]
    shared = {
        "wqkvT": np.ascontiguousarray(Wp.T),
        "bqkv": np.ascontiguousarray(bp),
        "woT": np.ascontiguousarray(W_o.T),
        "bo": np.ascontiguousarray(b_o),
        "wfcT": np.ascontiguousarray(w_fc.T),
        "wprojT": np.ascontiguousarray(w_proj.T),
        "wfcN": np.ascontiguousarray(w_fc),
        "wprojN": np.ascontiguousarray(w_proj),
        "uT": np.ascontiguousarray(u.T),
        "v16": np.ascontiguousarray(v),
        "fbT": np.ascontiguousarray(fb.T),
    }
    kl = np.arange(128)[:, None]
    ql = np.arange(512)[None, :]
    m = np.concatenate(
        [np.where(ql >= 128 * r + kl, 0.0, -1.0e30).astype(np.float32)
         for r in range(4)], axis=1)
    shared["masks"] = np.ascontiguousarray(m)
    shared["ident"] = np.eye(128, dtype=np.float32)

    in_maps = []
    for c in range(8):
        im = dict(shared)
        xc = x[2 * c: 2 * c + 2]
        im["xT"] = np.ascontiguousarray(xc.transpose(0, 2, 1))
        in_maps.append(im)
    return in_maps, gate


def kernel(**inputs):
    in_maps, gate = _prep_inputs(inputs)
    mk = MK(gate)
    nc = mk.build()
    res = run_bass_kernel_spmd(nc, in_maps, core_ids=list(range(8)))
    outs = []
    for c in range(8):
        o = res.results[c]["outT"]
        outs.append(np.ascontiguousarray(o.transpose(0, 2, 1)))
    return np.concatenate(outs, axis=0)
